# version 21
# baseline (speedup 1.0000x reference)
import numpy as np
import concourse.bass as bass
import concourse.mybir as mybir
from concourse.bass_utils import run_bass_kernel_spmd

F32 = mybir.dt.float32
F32R = mybir.dt.float32r
AX = mybir.AxisListType.X
OP = mybir.AluOpType
AF = mybir.ActivationFunctionType

NCORES = 8
D = 1024
TOK_CORE = 4096
TC = 512
NCHUNK = TOK_CORE // TC
KC = D // 128
NPAIR = 8
NGRP = TC // 64
CACHE_W = 576
RING_W = 152
NEG = -1.0e30
_STAGE = 99
_DUMP = False


def _ring_slice(g, lvl):
    if lvl == 1:
        return 8 * g, 8 * g + 8
    if lvl == 2:
        return 72 + 8 * (g // 2), 72 + 8 * (g // 2) + 8
    if lvl == 3:
        return 112 + 8 * (g // 4), 112 + 8 * (g // 4) + 8
    return 136, 144


class Sched:

    ENGS = ("tensor", "vector", "scalar", "sync", "gpsimd")

    def __init__(self, sems):
        self.streams = {e: [] for e in self.ENGS}
        self.sems = sems
        self.counts = {k: 0 for k in sems}
        self.last_wait = {}

    def wait(self, eng, sem, val):
        if val <= 0:
            return
        key = (eng, sem)
        if self.last_wait.get(key, 0) >= val:
            return
        self.last_wait[key] = val
        s = self.sems[sem]
        self.streams[eng].append(lambda e, s=s, v=val: e.wait_ge(s, v))

    def op(self, eng, fn, inc=None, inc_by=1):
        if inc is None:
            self.streams[eng].append(fn)
            return None
        s = self.sems[inc]
        self.counts[inc] += inc_by
        val = self.counts[inc]
        self.streams[eng].append(lambda e, fn=fn, s=s, k=inc_by: fn(e).then_inc(s, k))
        self.last_wait[(eng, inc)] = max(self.last_wait.get((eng, inc), 0), val)
        return val

    def wait_force(self, eng, sem, val):
        s = self.sems[sem]
        self.streams[eng].append(lambda e, s=s, v=val: e.wait_ge(s, v))

    def c(self, sem):
        return self.counts[sem]


def build_nc():
    nc = bass.Bass("TRN2", target_bir_lowering=False, detect_race_conditions=False)

    x_in = nc.dram_tensor("x", [D, TOK_CORE + TC], F32, kind="ExternalInput").ap()
    wq_in = nc.dram_tensor("wq", [3, 8, 128, 8, 128], F32, kind="ExternalInput").ap()
    wp_in = nc.dram_tensor("wp", [1, 8, 128, 8, 128], F32, kind="ExternalInput").ap()
    wf_in = nc.dram_tensor("wf", [8, 128, 8, 128], F32, kind="ExternalInput").ap()
    battn_in = nc.dram_tensor("battn", [128, 24], F32, kind="ExternalInput").ap()
    bproj_in = nc.dram_tensor("bproj", [128, 8], F32, kind="ExternalInput").ap()
    bffn_in = nc.dram_tensor("bffn", [128, 8], F32, kind="ExternalInput").ap()
    alpha_in = nc.dram_tensor("alpha", [128, 1], F32, kind="ExternalInput").ap()
    flag_in = nc.dram_tensor("flag", [128, 1], F32, kind="ExternalInput").ap()
    amask_in = nc.dram_tensor("amask", [128, 160], F32, kind="ExternalInput").ap()
    wcol_in = nc.dram_tensor("wcol", [128, 160], F32, kind="ExternalInput").ap()
    ident_in = nc.dram_tensor("ident", [128, 128], F32, kind="ExternalInput").ap()
    y_out = nc.dram_tensor("y", [D, TOK_CORE], F32, kind="ExternalOutput").ap()
    dbg_out = None
    if _DUMP:
        dbg_out = nc.dram_tensor("dbg", [128, 26624], F32, kind="ExternalOutput").ap()

    import contextlib
    with contextlib.ExitStack() as _stack:
        xbuf = _stack.enter_context(nc.sbuf_tensor("xbuf", [128, 4096], F32))
        lnbuf = _stack.enter_context(nc.sbuf_tensor("lnbuf", [128, 4096], F32))
        x1buf = _stack.enter_context(nc.sbuf_tensor("x1buf", [128, 4096], F32))
        qbuf = _stack.enter_context(nc.sbuf_tensor("qbuf", [128, 4096], F32))
        kcache = _stack.enter_context(nc.sbuf_tensor("kcache", [128, 8 * CACHE_W], F32))
        vcache = _stack.enter_context(nc.sbuf_tensor("vcache", [128, 8 * CACHE_W], F32))
        krings = _stack.enter_context(nc.sbuf_tensor("krings", [128, 8 * RING_W], F32))
        vrings = _stack.enter_context(nc.sbuf_tensor("vrings", [128, 8 * RING_W], F32))
        hibuf = _stack.enter_context(nc.sbuf_tensor("hibuf", [128, 1024], F32R))
        lobuf = _stack.enter_context(nc.sbuf_tensor("lobuf", [128, 1024], F32R))
        sqbuf = _stack.enter_context(nc.sbuf_tensor("sqbuf", [128, 1024], F32))
        wtern = _stack.enter_context(nc.sbuf_tensor("wtern", [128, 8192], F32R))
        panel = _stack.enter_context(nc.sbuf_tensor("panel", [128, 2048], F32))
        whilo = _stack.enter_context(nc.sbuf_tensor("whilo", [128, 4096], F32R))
        battn = _stack.enter_context(nc.sbuf_tensor("sb_battn", [128, 24], F32))
        bproj = _stack.enter_context(nc.sbuf_tensor("sb_bproj", [128, 8], F32))
        bffn = _stack.enter_context(nc.sbuf_tensor("sb_bffn", [128, 8], F32))
        alpha_t = _stack.enter_context(nc.sbuf_tensor("alpha_t", [128, 1], F32))
        flag_t = _stack.enter_context(nc.sbuf_tensor("flag_t", [128, 1], F32))
        amask = _stack.enter_context(nc.sbuf_tensor("sb_amask", [128, 160], F32))
        wcolt = _stack.enter_context(nc.sbuf_tensor("wcolt", [128, 160], F32))
        ident = _stack.enter_context(nc.sbuf_tensor("sb_ident", [128, 128], F32))
        halpha = _stack.enter_context(nc.sbuf_tensor("halpha", [128, 1], F32))
        hbffn = _stack.enter_context(nc.sbuf_tensor("hbffn", [128, 8], F32))
        battn_q = _stack.enter_context(nc.sbuf_tensor("battn_q", [128, 8], F32))
        bkv_fl = _stack.enter_context(nc.sbuf_tensor("bkv_fl", [128, 16], F32))
        ones = _stack.enter_context(nc.sbuf_tensor("ones", [128, 1], F32))
        onesr = _stack.enter_context(nc.sbuf_tensor("onesr", [1, 128], F32))
        stats = _stack.enter_context(nc.sbuf_tensor("stats", [1, 1536], F32))
        dquant = _stack.enter_context(nc.sbuf_tensor("dquant", [128, 2], F32))
        epst = _stack.enter_context(nc.sbuf_tensor("epst", [1, 1], F32))
        epsb = _stack.enter_context(nc.sbuf_tensor("epsb", [128, 1], F32))
        absacc = _stack.enter_context(nc.sbuf_tensor("absacc", [128, 8], F32))
        terntmp = _stack.enter_context(nc.sbuf_tensor("terntmp", [128, 1024], F32))
        bdq = _stack.enter_context(nc.sbuf_tensor("bdq", [128, 384], F32))
        vcomb = _stack.enter_context(nc.sbuf_tensor("vcomb", [128, 192], F32))
        kcomb = _stack.enter_context(nc.sbuf_tensor("kcomb", [128, 128], F32))
        sq8 = _stack.enter_context(nc.sbuf_tensor("sq8", [128, 1024], F32))
        mskS = _stack.enter_context(nc.sbuf_tensor("mskS", [128, 320], F32))
        ebuf = _stack.enter_context(nc.sbuf_tensor("ebuf", [128, 320], F32))
        eprb = _stack.enter_context(nc.sbuf_tensor("eprb", [128, 320], F32))
        cbuf = _stack.enter_context(nc.sbuf_tensor("cbuf", [128, 320], F32))
        small = _stack.enter_context(nc.sbuf_tensor("small", [128, 8], F32))
        cts = _stack.enter_context(nc.sbuf_tensor("cts", [128, 1024], F32))
        ps0 = _stack.enter_context(nc.psum_tensor("ps0", [128, 512], F32))
        ps1 = _stack.enter_context(nc.psum_tensor("ps1", [128, 512], F32))
        ps2 = _stack.enter_context(nc.psum_tensor("ps2", [128, 512], F32))
        ps3 = _stack.enter_context(nc.psum_tensor("ps3", [128, 512], F32))
        ps4 = _stack.enter_context(nc.psum_tensor("ps4", [128, 512], F32))
        ps5 = _stack.enter_context(nc.psum_tensor("ps5", [128, 512], F32))
        ps6 = _stack.enter_context(nc.psum_tensor("ps6", [128, 512], F32))
        ps7 = _stack.enter_context(nc.psum_tensor("ps7", [128, 512], F32))
        semT = _stack.enter_context(nc.semaphore("semT"))
        semV = _stack.enter_context(nc.semaphore("semV"))
        semS = _stack.enter_context(nc.semaphore("semS"))
        semX = _stack.enter_context(nc.semaphore("semX"))
        semC = _stack.enter_context(nc.semaphore("semC"))
        semW0 = _stack.enter_context(nc.semaphore("semW0"))
        semW1 = _stack.enter_context(nc.semaphore("semW1"))
        semO = _stack.enter_context(nc.semaphore("semO"))
        block = _stack.enter_context(nc.Block())
        ps = [ps0, ps1, ps2, ps3, ps4, ps5, ps6, ps7]
        S = Sched({"T": semT, "V": semV, "S": semS, "X": semX, "C": semC,
                   "W0": semW0, "W1": semW1, "O": semO})

        S.op("sync", lambda e: e.dma_start(out=battn[:], in_=battn_in), inc="C", inc_by=16)
        S.op("sync", lambda e: e.dma_start(out=bproj[:], in_=bproj_in), inc="C", inc_by=16)
        S.op("sync", lambda e: e.dma_start(out=bffn[:], in_=bffn_in), inc="C", inc_by=16)
        S.op("sync", lambda e: e.dma_start(out=alpha_t[:], in_=alpha_in), inc="C", inc_by=16)
        S.op("sync", lambda e: e.dma_start(out=flag_t[:], in_=flag_in), inc="C", inc_by=16)
        S.op("sync", lambda e: e.dma_start(out=amask[:], in_=amask_in), inc="C", inc_by=16)
        S.op("sync", lambda e: e.dma_start(out=wcolt[:], in_=wcol_in), inc="C", inc_by=16)
        S.op("sync", lambda e: e.dma_start(out=ident[:], in_=ident_in), inc="C", inc_by=16)
        consts_c = S.c("C")

        S.wait("vector", "C", consts_c)
        S.op("vector", lambda e: e.memset(ones[:], 1.0))
        S.op("vector", lambda e: e.memset(epst[:], 1e-5))
        S.op("vector", lambda e: e.memset(epsb[:], 1e-5))
        if _DUMP:
            for buf in (lnbuf, x1buf, qbuf, kcache, vcache):
                S.op("vector", lambda e, buf=buf: e.memset(buf[:], 0.0))
        S.op("vector", lambda e: e.memset(onesr[:], 1.0))
        S.op("vector", lambda e: e.memset(bdq[:], 0.0))
        S.op("vector", lambda e: e.tensor_scalar_mul(halpha[:], alpha_t[:], 0.5))
        S.op("vector", lambda e: e.tensor_scalar_mul(hbffn[:], bffn[:], 0.5))
        S.op("vector", lambda e: e.tensor_scalar_mul(battn_q[:], battn[:, 0:8], 0.125))
        S.op("vector", lambda e: e.tensor_scalar_mul(bkv_fl[:], battn[:, 8:24], flag_t[:, 0:1]),
             inc="V")

        qrel = [("V", 0), ("V", 0)]
        for p in range(8):
            sl = p % 2
            S.wait("sync", *qrel[sl])
            S.op("sync", lambda e, p=p, sl=sl: e.dma_start(
                out=panel[:, 1024 * sl:1024 * sl + 1024],
                in_=wf_in[p].rearrange("p k m -> p (k m)")), inc="W%d" % sl, inc_by=16)
            S.wait("vector", "W%d" % sl, S.c("W%d" % sl))
            vv = S.op("vector", lambda e, p=p, sl=sl: e.reduce_sum(
                absacc[:, p:p + 1], panel[:, 1024 * sl:1024 * sl + 1024],
                axis=AX, apply_absolute_value=True), inc="V")
            qrel[sl] = ("V", vv)
        v_abs = S.c("V")
        S.wait("tensor", "V", v_abs)
        S.op("tensor", lambda e: e.matmul(ps0[0:1, 0:8], ones[:, 0:1], absacc[:], start=True, stop=True), inc="T")
        S.wait("vector", "T", S.c("T"))
        vv = S.op("vector", lambda e: e.reduce_sum(stats[0:1, 0:1], ps0[0:1, 0:8], axis=AX), inc="V")
        S.wait_force("vector", "V", vv)
        S.op("vector", lambda e: e.tensor_scalar_mul(stats[0:1, 4:5], stats[0:1, 0:1], 0.7 / (1024.0 * 1024.0)),
             inc="V")
        S.wait("tensor", "V", S.c("V"))
        S.op("tensor", lambda e: e.matmul(ps1[:, 0:1], onesr[:], stats[0:1, 4:5], start=True, stop=True), inc="T")
        S.wait("vector", "T", S.c("T"))
        S.op("vector", lambda e: e.tensor_copy(dquant[:, 0:1], ps1[:, 0:1]), inc="V")
        S.wait_force("vector", "V", S.c("V"))
        S.op("vector", lambda e: e.tensor_scalar_mul(dquant[:, 1:2], dquant[:, 0:1], -1.0), inc="V")
        S.wait_force("vector", "V", S.c("V"))
        wtern3 = wtern[:].rearrange("p (k m) -> p k m", k=8)
        for p in range(8):
            sl = p % 2
            S.wait("sync", *qrel[sl])
            S.op("sync", lambda e, p=p, sl=sl: e.dma_start(
                out=panel[:, 1024 * sl:1024 * sl + 1024],
                in_=wf_in[p].rearrange("p k m -> p (k m)")), inc="W%d" % sl, inc_by=16)
            S.wait("vector", "W%d" % sl, S.c("W%d" % sl))
            for h in range(2):
                S.op("vector", lambda e, sl=sl, h=h: e.tensor_single_scalar(
                    terntmp[:, 0:512], panel[:, 1024 * sl + 512 * h:1024 * sl + 512 * h + 512],
                    dquant[:, 0:1], OP.is_gt))
                S.op("vector", lambda e, sl=sl, h=h: e.tensor_single_scalar(
                    terntmp[:, 512:1024], panel[:, 1024 * sl + 512 * h:1024 * sl + 512 * h + 512],
                    dquant[:, 1:2], OP.is_lt))
                S.op("vector", lambda e, p=p, h=h: e.tensor_sub(
                    wtern3[:, 4 * h:4 * h + 4, 128 * p:128 * p + 128],
                    terntmp[:, 0:512].rearrange("p (k m) -> p k m", k=4),
                    terntmp[:, 512:1024].rearrange("p (k m) -> p k m", k=4)), inc="V")
            qrel[sl] = ("V", S.c("V"))
        v_init_done = S.c("V")

        rel = {
            "xbuf": ("V", 0), "x1buf": ("O", 0),
            "pan0": ("V", v_init_done), "pan1": ("V", v_init_done),
            "ps0": ("V", v_init_done), "ps1": ("V", v_init_done),
            "ps2": ("V", 0), "ps3": ("V", 0), "ps4": ("V", 0), "ps5": ("V", 0),
            "ps6": ("V", 0), "ps7": ("V", 0),
            "hilo0": ("T", 0), "hilo1": ("T", 0),
            "whilo0": ("T", 0), "whilo1": ("T", 0),
            "sq0": ("T", 0), "sq1": ("T", 0),
        }
        att_bd = [("T", 0), ("T", 0)]
        att_msk = [("S", 0), ("S", 0)]
        att_cb = [("T", 0), ("T", 0)]

        def emit_x_load(cidx):
            t0 = TC * (cidx + 1)
            sem, val = rel["xbuf"]
            S.wait("sync", sem, val)
            for k in range(KC):
                S.op("sync", lambda e, k=k, t0=t0: e.dma_start(
                    out=xbuf[:, 512 * k:512 * k + 512],
                    in_=x_in[128 * k:128 * k + 128, t0:t0 + 512]), inc="X", inc_by=16)
            return S.c("X")

        x_ready = emit_x_load(-1)

        def emit_layernorm(src, dst, src_ready=None, post_k=None):
            if src_ready is None:
                S.wait("tensor", "V", S.c("V"))
            S.wait("tensor", *rel["ps6"])
            for k in range(KC):
                if src_ready is not None:
                    S.wait("tensor", *src_ready[k])
                S.op("tensor", lambda e, k=k: e.matmul(
                    ps6[0:1, :], ones[:, 0:1], src[:, 512 * k:512 * k + 512],
                    start=(k == 0), stop=(k == KC - 1)))
            S.wait("tensor", *rel["ps7"])
            def sqslot(k):
                base = sq8 if (k % 4) < 2 else terntmp
                return base[:, 512 * (k % 2):512 * (k % 2) + 512]
            sq_t = {}
            t_stats = None
            for k in range(KC):
                if k - 4 in sq_t:
                    S.wait("vector", "T", sq_t[k - 4])
                vv = S.op("vector", lambda e, k=k: e.tensor_mul(
                    sqslot(k), src[:, 512 * k:512 * k + 512],
                    src[:, 512 * k:512 * k + 512]), inc="V")
                S.wait("tensor", "V", vv)
                t_stats = sq_t[k] = S.op("tensor", lambda e, k=k: e.matmul(
                    ps7[0:1, :], ones[:, 0:1], sqslot(k),
                    start=(k == 0), stop=(k == KC - 1)), inc="T")
            S.wait("scalar", "T", t_stats)
            S.op("scalar", lambda e: e.activation(stats[0:1, 0:512], ps6[0:1, :], AF.Identity))
            s_rows = S.op("scalar", lambda e: e.activation(stats[0:1, 512:1024], ps7[0:1, :], AF.Identity), inc="S")
            rel["ps6"] = ("S", s_rows)
            rel["ps7"] = ("S", s_rows)
            S.wait("tensor", "S", s_rows)
            S.wait("tensor", *rel["ps5"])
            S.wait("tensor", *rel["ps4"])
            S.op("tensor", lambda e: e.matmul(ps5[:, :], onesr[:], stats[0:1, 0:512], start=True, stop=True))
            t_bc = S.op("tensor", lambda e: e.matmul(ps4[:, :], onesr[:], stats[0:1, 512:1024], start=True, stop=True),
                        inc="T")
            A = sqbuf[:, 0:512]
            B = sqbuf[:, 512:1024]
            S.wait("scalar", "T", t_bc)
            s_sq = S.op("scalar", lambda e: e.activation(A, ps5[:, :], AF.Square, scale=1.0 / 1024.0), inc="S")
            S.wait("vector", "S", s_sq)
            v_var = S.op("vector", lambda e: e.scalar_tensor_tensor(
                B, ps4[:, :], 1.0 / 1024.0, A, OP.mult, OP.subtract), inc="V")
            S.wait("scalar", "V", v_var)
            S.op("scalar", lambda e: e.activation(A, B, AF.Ln, bias=epsb[:, 0:1]))
            s_r = S.op("scalar", lambda e: e.activation(B, A, AF.Exp, scale=-0.5), inc="S")
            S.wait("vector", "S", s_r)
            v_mr = S.op("vector", lambda e: e.scalar_tensor_tensor(
                A, ps5[:, :], 1.0 / 1024.0, B, OP.mult, OP.mult), inc="V")
            rel["ps4"] = ("V", v_mr)
            rel["ps5"] = ("V", v_mr)
            gates = []
            for k in range(KC):
                S.op("vector", lambda e, k=k: e.tensor_mul(dst[:, 512 * k:512 * k + 512],
                                                           src[:, 512 * k:512 * k + 512], B))
                vv = S.op("vector", lambda e, k=k: e.tensor_sub(dst[:, 512 * k:512 * k + 512],
                                                                dst[:, 512 * k:512 * k + 512], A),
                     inc="V")
                gates.append(("V", vv))
                if post_k is not None and k < 2:
                    post_k(k)
            return gates

        def sqslot2(k):
            base = sq8 if (k % 4) < 2 else terntmp
            return base[:, 512 * (k % 2):512 * (k % 2) + 512]

        def emit_ln_stats2(src, src_ready):
            S.wait("tensor", *rel["ps6"])
            for k in range(KC):
                if src_ready is not None:
                    S.wait("tensor", *src_ready[k])
                S.op("tensor", lambda e, k=k: e.matmul(
                    ps6[0:1, :], ones[:, 0:1], src[:, 512 * k:512 * k + 512],
                    start=(k == 0), stop=(k == KC - 1)))
            S.wait("tensor", *rel["ps7"])
            sq_t = {}
            t_stats = None
            for k in range(KC):
                if k - 4 in sq_t:
                    S.wait("vector", "T", sq_t[k - 4])
                vv = S.op("vector", lambda e, k=k: e.tensor_mul(
                    sqslot2(k), src[:, 512 * k:512 * k + 512],
                    src[:, 512 * k:512 * k + 512]), inc="V")
                S.wait("tensor", "V", vv)
                t_stats = sq_t[k] = S.op("tensor", lambda e, k=k: e.matmul(
                    ps7[0:1, :], ones[:, 0:1], sqslot2(k),
                    start=(k == 0), stop=(k == KC - 1)), inc="T")
            S.wait("vector", "T", t_stats)
            S.op("vector", lambda e: e.tensor_scalar_mul(stats[0:1, 0:512], ps6[0:1, :], 1.0 / 1024.0))
            S.op("vector", lambda e: e.tensor_mul(stats[0:1, 512:1024], stats[0:1, 0:512], stats[0:1, 0:512]))
            v_m = S.op("vector", lambda e: e.scalar_tensor_tensor(
                stats[0:1, 1024:1536], ps7[0:1, :], 1.0 / 1024.0, stats[0:1, 512:1024],
                OP.mult, OP.subtract), inc="V")
            rel["ps6"] = ("V", v_m)
            rel["ps7"] = ("V", v_m)
            S.wait("scalar", "V", v_m)
            S.op("scalar", lambda e: e.activation(stats[0:1, 512:1024], stats[0:1, 1024:1536], AF.Ln, bias=epst[0:1, 0:1]))
            s_r = S.op("scalar", lambda e: e.activation(cts[0:1, 0:512], stats[0:1, 512:1024], AF.Exp, scale=-0.5),
                       inc="S")
            S.wait("vector", "S", s_r)
            v_mr = S.op("vector", lambda e: e.tensor_mul(
                cts[0:1, 512:1024], stats[0:1, 0:512], cts[0:1, 0:512]), inc="V")
            return v_mr

        def emit_ln_center2(src, dst, v_mr, post_k=None):
            S.wait("tensor", "V", v_mr)
            S.wait("tensor", *rel["ps6"])
            S.wait("tensor", *rel["ps7"])
            S.op("tensor", lambda e: e.matmul(ps6[:, :], onesr[:], cts[0:1, 0:512], start=True, stop=True))
            t_bc = S.op("tensor", lambda e: e.matmul(ps7[:, :], onesr[:], cts[0:1, 512:1024], start=True, stop=True),
                        inc="T")
            S.wait("vector", "T", t_bc)
            gates = []
            for k in range(KC):
                sq = sqbuf[:, 512 * (k % 2):512 * (k % 2) + 512]
                S.op("vector", lambda e, k=k, sq=sq: e.tensor_mul(sq, src[:, 512 * k:512 * k + 512], ps6[:, :]))
                vv = S.op("vector", lambda e, k=k, sq=sq: e.tensor_sub(dst[:, 512 * k:512 * k + 512], sq, ps7[:, :]),
                     inc="V")
                gates.append(("V", vv))
                if post_k is not None and k < 2:
                    post_k(k)
            v_ln = S.c("V")
            rel["ps6"] = ("V", v_ln)
            rel["ps7"] = ("V", v_ln)
            return gates

        def make_presplit(src, store):
            def post_k(k):
                pr = k % 2
                hi = hibuf[:, 512 * pr:512 * pr + 512]
                lo = lobuf[:, 512 * pr:512 * pr + 512]
                S.wait("vector", *rel["hilo%d" % pr])
                S.op("vector", lambda e, k=k, hi=hi: e.tensor_copy(hi, src[:, 512 * k:512 * k + 512]))
                store[k] = S.op("vector", lambda e, k=k, hi=hi, lo=lo: e.tensor_sub(
                    lo, src[:, 512 * k:512 * k + 512], hi), inc="V")
            return post_k

        def emit_ffn(src, resid, dst, pre_split=None):
            for i in range(8):
                S.wait("tensor", *rel["ps%d" % i])
            bank_t = {}
            for k in range(KC):
                pr = k % 2
                hi = hibuf[:, 512 * pr:512 * pr + 512]
                lo = lobuf[:, 512 * pr:512 * pr + 512]
                if pre_split is not None and k in pre_split:
                    S.wait("tensor", "V", pre_split[k])
                else:
                    S.wait("vector", *rel["hilo%d" % pr])
                    S.op("vector", lambda e, k=k, hi=hi: e.tensor_copy(hi, src[:, 512 * k:512 * k + 512]))
                    S.op("vector", lambda e, k=k, hi=hi, lo=lo: e.tensor_sub(lo, src[:, 512 * k:512 * k + 512], hi),
                         inc="V")
                    S.wait("tensor", "V", S.c("V"))
                last = (k == KC - 1)
                first = (k == 0)
                tval = None
                for m in range(8):
                    wt = wtern[:, 1024 * k + 128 * m:1024 * k + 128 * m + 128]
                    S.op("tensor", lambda e, m=m, wt=wt, hi=hi, first=first: e.matmul(
                        ps[m][:, :], wt, hi, start=first, stop=False))
                    tval = S.op("tensor", lambda e, m=m, wt=wt, lo=lo, last=last: e.matmul(
                        ps[m][:, :], wt, lo, start=False, stop=last),
                        inc="T" if (m == 7 or last) else None)
                    if last:
                        bank_t[m] = tval
                rel["hilo%d" % pr] = ("T", tval)
            t_ffn = S.c("T")
            S.wait("vector", "T", t_ffn)
            sem, val = rel["x1buf"] if dst is x1buf else ("V", 0)
            if val:
                S.wait("vector", sem, val)
            evac_gates = []
            evac_vv = {}
            for m in range(8):
                sq = sqbuf[:, 512 * (m % 2):512 * (m % 2) + 512]
                S.wait("scalar", "T", bank_t[m])
                if m - 2 in evac_vv:
                    S.wait("scalar", "V", evac_vv[m - 2])
                sv = S.op("scalar", lambda e, m=m, sq=sq: e.activation(
                    sq, ps[m][:, :], AF.Identity, bias=hbffn[:, m:m + 1], scale=halpha[:, 0:1]), inc="S")
                S.wait("vector", "S", sv)
                vv = S.op("vector", lambda e, m=m, sq=sq: e.tensor_add(
                    dst[:, 512 * m:512 * m + 512], sq, resid[:, 512 * m:512 * m + 512]), inc="V")
                evac_vv[m] = vv
                evac_gates.append(("V", vv))
            v_ffn = S.c("V")
            for i in range(8):
                rel["ps%d" % i] = ("V", v_ffn)
            rel["sq0"] = ("V", v_ffn)
            rel["sq1"] = ("V", v_ffn)
            return evac_gates

        def emit_wmm(w_in, groups, src, evac_fn, v_src, src_gates=None, pre_split=None,
                     filler_factory=None, gates_out=None):
            gates = gates_out if gates_out is not None else {}
            slab_done = {}

            def issue_slab(g, k, si):
                sl = si % 2
                S.wait("sync", *rel["pan%d" % sl])
                S.op("sync", lambda e, g=g, k=k, sl=sl: e.dma_start(
                    out=panel[:, 1024 * sl:1024 * sl + 1024],
                    in_=w_in[g, k].rearrange("p k m -> p (k m)")), inc="W%d" % sl, inc_by=16)
                slab_done[(g, k)] = S.c("W%d" % sl)

            seq = [(g, k) for g in groups for k in range(KC)]
            issue_slab(*seq[0], 0)
            if len(seq) > 1:
                issue_slab(*seq[1], 1)
            filler = None
            for si, (g, k) in enumerate(seq):
                sl = si % 2
                if filler_factory is not None and g == groups[-1] and k == 0:
                    filler = filler_factory()
                    filler_factory = None
                S.wait("vector", "W%d" % sl, slab_done[(g, k)])
                S.wait("vector", *rel["whilo%d" % sl])
                S.op("vector", lambda e, sl=sl: e.tensor_copy(
                    whilo[:, 2048 * sl:2048 * sl + 1024], panel[:, 1024 * sl:1024 * sl + 1024]))
                v_wsplit = S.op("vector", lambda e, sl=sl: e.tensor_sub(
                    whilo[:, 2048 * sl + 1024:2048 * sl + 2048], panel[:, 1024 * sl:1024 * sl + 1024],
                    whilo[:, 2048 * sl:2048 * sl + 1024]), inc="V")
                pr = k % 2
                hi = hibuf[:, 512 * pr:512 * pr + 512]
                lo = lobuf[:, 512 * pr:512 * pr + 512]
                if pre_split is not None and si < 2 and k in pre_split:
                    if k == 0:
                        for i in range(8):
                            S.wait("tensor", *rel["ps%d" % i])
                    S.wait("tensor", "V", max(pre_split[k], v_wsplit))
                else:
                    S.wait("vector", *rel["hilo%d" % pr])
                    if src_gates is not None:
                        S.wait("vector", *src_gates[k])
                    else:
                        S.wait("vector", "V", v_src)
                    S.op("vector", lambda e, k=k, hi=hi: e.tensor_copy(hi, src[:, 512 * k:512 * k + 512]))
                    S.op("vector", lambda e, k=k, hi=hi, lo=lo: e.tensor_sub(
                        lo, src[:, 512 * k:512 * k + 512], hi), inc="V")
                    if k == 0:
                        for i in range(8):
                            S.wait("tensor", *rel["ps%d" % i])
                    S.wait("tensor", "V", S.c("V"))
                tval = None
                last = (k == KC - 1)
                bank_t = {}
                for p in range(8):
                    whi = whilo[:, 2048 * sl + 128 * p:2048 * sl + 128 * p + 128]
                    wlo = whilo[:, 2048 * sl + 1024 + 128 * p:2048 * sl + 1024 + 128 * p + 128]
                    S.op("tensor", lambda e, p=p, whi=whi, hi=hi, k=k: e.matmul(
                        ps[p][:, :], whi, hi, start=(k == 0), stop=False))
                    S.op("tensor", lambda e, p=p, wlo=wlo, hi=hi: e.matmul(
                        ps[p][:, :], wlo, hi, start=False, stop=False))
                    tval = S.op("tensor", lambda e, p=p, whi=whi, lo=lo, last=last: e.matmul(
                        ps[p][:, :], whi, lo, start=False, stop=last),
                        inc="T" if (p == 7 or last) else None)
                    if last:
                        bank_t[p] = tval
                rel["hilo%d" % pr] = ("T", tval)
                rel["whilo%d" % sl] = ("T", tval)
                rel["pan%d" % sl] = ("V", v_wsplit)
                if filler:
                    for _ in range(min(18, len(filler))):
                        filler.pop(0)()
                if si + 2 < len(seq):
                    issue_slab(*seq[si + 2], si + 2)
                if k == KC - 1:
                    for p in range(8):
                        gates[8 * g + p] = evac_fn(g, p, ps[p], bank_t[p])
            return gates, filler

        def emit_qkv(warm, ln2_gates=None, qkv_pre=None, filler_factory=None, gates_out=None):
            groups = (1, 2) if warm else (1, 2, 0)
            v_ln2 = S.c("V")

            def evac(g, p, bank, tval):
                pid = 8 * g + p
                S.wait("scalar", "T", tval)
                if pid < 8:
                    sval = S.op("scalar", lambda e, pid=pid, bank=bank: e.activation(
                        qbuf[:, 512 * pid:512 * pid + 512], bank[:, :], AF.Identity,
                        bias=battn_q[:, pid:pid + 1], scale=0.125), inc="S")
                else:
                    cache = kcache if pid < 16 else vcache
                    j = (pid - 8) % 8
                    if warm:
                        sval = S.op("scalar", lambda e, bank=bank, cache=cache, j=j, pid=pid: e.activation(
                            cache[:, CACHE_W * j + 64:CACHE_W * j + 576], bank[:, :], AF.Identity,
                            bias=bkv_fl[:, pid - 8:pid - 7], scale=flag_t[:, 0:1]), inc="S")
                    else:
                        sval = S.op("scalar", lambda e, bank=bank, cache=cache, j=j, pid=pid: e.activation(
                            cache[:, CACHE_W * j + 64:CACHE_W * j + 576], bank[:, :], AF.Identity,
                            bias=battn[:, pid:pid + 1]), inc="S")
                rel["ps%d" % p] = ("S", sval)
                return ("S", sval)

            return emit_wmm(wq_in, groups, lnbuf, evac, v_ln2, src_gates=ln2_gates, pre_split=qkv_pre,
                            filler_factory=filler_factory, gates_out=gates_out)

        def make_pooling_thunks(kv_gates=None):
            thunks = []

            def T(fn):
                thunks.append(fn)

            s_qkv = S.c("S")
            combos = [(p, kv) for kv in range(2) for p in range(NPAIR)]
            cach = (kcache, vcache)
            rngs = (krings, vrings)

            def h1ap(i):
                j = i % 8
                base = sq8 if j < 4 else terntmp
                return base[:, 256 * (j % 4):256 * (j % 4) + 256]

            def h2ap(i):
                return cts[:, 128 * (i % 8):128 * (i % 8) + 128]

            for half in (0, 1):
                for i in range(8 * half, 8 * half + 8):
                    p, kv = combos[i]
                    cur = cach[kv][:, CACHE_W * p + 64:CACHE_W * p + 576]

                    def t_h1(i=i, p=p, kv=kv, cur=cur):
                        if kv_gates is not None:
                            S.wait("vector", *kv_gates[(8 if kv == 0 else 16) + p])
                        else:
                            S.wait("vector", "S", s_qkv)
                        S.op("vector", lambda e, cur=cur, i=i: e.tensor_add(
                            h1ap(i), cur[:, 0:512:2], cur[:, 1:512:2]))
                    T(t_h1)
                for i in range(8 * half, 8 * half + 8):
                    T(lambda i=i: S.op("vector", lambda e, i=i: e.tensor_add(
                        h2ap(i), h1ap(i)[:, 0:256:2], h1ap(i)[:, 1:256:2])))
                for i in range(8 * half, 8 * half + 8):
                    p, kv = combos[i]
                    r0 = RING_W * p
                    T(lambda i=i, ring=rngs[kv], r0=r0: S.op("vector", lambda e, i=i, ring=ring, r0=r0: e.tensor_add(
                        ring[:, r0 + 8:r0 + 72], h2ap(i)[:, 0:128:2], h2ap(i)[:, 1:128:2])))
            for i, (p, kv) in enumerate(combos):
                r0 = RING_W * p
                T(lambda ring=rngs[kv], r0=r0: S.op("vector", lambda e, ring=ring, r0=r0: e.tensor_add(
                    ring[:, r0 + 80:r0 + 112], ring[:, r0 + 8:r0 + 72:2], ring[:, r0 + 9:r0 + 72:2])))
            for i, (p, kv) in enumerate(combos):
                r0 = RING_W * p
                T(lambda ring=rngs[kv], r0=r0: S.op("vector", lambda e, ring=ring, r0=r0: e.tensor_add(
                    ring[:, r0 + 120:r0 + 136], ring[:, r0 + 80:r0 + 112:2], ring[:, r0 + 81:r0 + 112:2])))
            for i, (p, kv) in enumerate(combos):
                r0 = RING_W * p
                T(lambda ring=rngs[kv], r0=r0: S.op("vector", lambda e, ring=ring, r0=r0: e.tensor_add(
                    ring[:, r0 + 144:r0 + 152], ring[:, r0 + 120:r0 + 136:2], ring[:, r0 + 121:r0 + 136:2])))
            nsc = [0]
            for sc, lo_, hi_ in ((0.125, 8, 72), (1.0 / 16.0, 80, 112),
                                 (1.0 / 32.0, 120, 136), (1.0 / 64.0, 144, 152)):
                for p in range(NPAIR):
                    r0 = RING_W * p
                    nsc[0] += 1
                    last = (nsc[0] == 32)

                    def t_sc2(r0=r0, sc=sc, lo_=lo_, hi_=hi_, last=last):
                        S.op("vector", lambda e, r0=r0, sc=sc, lo_=lo_, hi_=hi_: e.tensor_scalar_mul(
                            krings[:, r0 + lo_:r0 + hi_], krings[:, r0 + lo_:r0 + hi_], sc),
                            inc="V" if last else None)
                    T(t_sc2)
            return thunks

        def emit_pooling(kv_gates=None, thunks=None):
            if thunks is None:
                thunks = make_pooling_thunks(kv_gates)
            while thunks:
                thunks.pop(0)()
            return S.c("V")

        def emit_ring_shift(t_gate=None):
            S.wait("vector", "T", t_gate if t_gate is not None else S.c("T"))
            for p in range(NPAIR):
                for ring in (krings, vrings):
                    r0 = RING_W * p
                    S.op("vector", lambda e, ring=ring, r0=r0: e.tensor_copy(ring[:, r0 + 0:r0 + 8], ring[:, r0 + 64:r0 + 72]))
                    S.op("vector", lambda e, ring=ring, r0=r0: e.tensor_copy(ring[:, r0 + 72:r0 + 80], ring[:, r0 + 104:r0 + 112]))
                    S.op("vector", lambda e, ring=ring, r0=r0: e.tensor_copy(ring[:, r0 + 112:r0 + 120], ring[:, r0 + 128:r0 + 136]))
                    S.op("vector", lambda e, ring=ring, r0=r0: e.tensor_copy(ring[:, r0 + 136:r0 + 144], ring[:, r0 + 144:r0 + 152]))
                for cache in (kcache, vcache):
                    c0 = CACHE_W * p
                    S.op("vector", lambda e, cache=cache, c0=c0: e.tensor_copy(
                        cache[:, c0:c0 + 64], cache[:, c0 + 512:c0 + 576]))
            return S.op("vector", lambda e: e.memset(small[:, 6:7], 0.0), inc="V")

        def emit_attention(kv_gates=None):
            s_qkv = S.c("S")
            NU = NPAIR * NGRP
            t_sc, v_msk, s_exp, v_es, s_c, t_tr, s_vt, t_val, s_out = ({} for _ in range(9))

            def st_stage(u):
                p, g = divmod(u, NGRP)
                sl = u % 3
                q0 = 512 * p + 64 * g
                sl8 = u % 6
                if kv_gates is not None:
                    S.wait("vector", *kv_gates[p])
                else:
                    S.wait("vector", "S", s_qkv)
                if u - 3 in t_sc:
                    S.wait("vector", "T", t_sc[u - 3])
                if u - 6 in t_tr:
                    S.wait("vector", "T", t_tr[u - 6])
                S.op("vector", lambda e, q0=q0, sl=sl: e.tensor_copy(
                    bdq[0:64, 128 * sl:128 * sl + 64], qbuf[0:64, q0:q0 + 64]))
                S.op("vector", lambda e, q0=q0, sl=sl: e.tensor_copy(
                    bdq[64:128, 128 * sl + 64:128 * sl + 128], qbuf[64:128, q0:q0 + 64]))
                for j, lvl in enumerate((1, 2, 3, 4)):
                    a, b = _ring_slice(g, lvl)
                    S.op("vector", lambda e, p=p, a=a, b=b, j=j, sl8=sl8: e.tensor_copy(
                        vcomb[:, 32 * sl8 + 8 * j:32 * sl8 + 8 * j + 8],
                        vrings[:, RING_W * p + a:RING_W * p + b]))
                vs = None
                for j, lvl in enumerate((1, 2, 3, 4)):
                    a, b = _ring_slice(g, lvl)
                    vs = S.op("vector", lambda e, p=p, a=a, b=b, j=j, sl=sl: e.tensor_copy(
                        kcomb[:, 32 * sl + 8 * j:32 * sl + 8 * j + 8],
                        krings[:, RING_W * p + a:RING_W * p + b]), inc="V" if j == 3 else None)
                return vs

            def psS_ap(u):
                return ps[u % 2][:, 256 * ((u // 2) % 2):256 * ((u // 2) % 2) + 160]

            def st_scores(u, v_stage):
                p, g = divmod(u, NGRP)
                sl = u % 3
                kw0 = CACHE_W * p + 64 * g
                if kv_gates is not None:
                    S.wait("tensor", *kv_gates[8 + p])
                S.wait("tensor", "V", v_stage)
                if u - 4 in v_msk:
                    S.wait("tensor", "V", v_msk[u - 4])
                pS = psS_ap(u)
                S.op("tensor", lambda e, sl=sl, pS=pS, kw0=kw0: e.matmul(
                    pS[:, 0:128], bdq[:, 128 * sl:128 * sl + 128], kcache[:, kw0:kw0 + 128],
                    start=True, stop=False))
                t_sc[u] = S.op("tensor", lambda e, sl=sl, pS=pS: e.matmul(
                    pS[:, 128:160], bdq[:, 128 * sl:128 * sl + 128],
                    kcomb[:, 32 * sl:32 * sl + 32],
                    start=False, stop=True), inc="T")

            def st_mask(u):
                pi = u % 2
                S.wait("vector", "T", t_sc[u])
                if u - 2 in s_exp:
                    S.wait("vector", "S", s_exp[u - 2])
                pS = psS_ap(u)
                S.op("vector", lambda e, pi=pi, pS=pS: e.tensor_add(
                    mskS[:, 160 * pi:160 * pi + 160], pS[:, 0:160], amask[:]))
                v_msk[u] = S.op("vector", lambda e, pi=pi: e.reduce_max(
                    small[:, pi:pi + 1], mskS[:, 160 * pi:160 * pi + 160], axis=AX, negate=True), inc="V")

            def st_exp(u):
                pi = u % 2
                S.wait("scalar", "V", v_msk[u])
                s_exp[u] = S.op("scalar", lambda e, pi=pi: e.activation(
                    ebuf[:, 160 * pi:160 * pi + 160], mskS[:, 160 * pi:160 * pi + 160], AF.Exp,
                    bias=small[:, pi:pi + 1]), inc="S")

            def st_eprc(u):
                pi = u % 2
                S.wait("vector", "S", s_exp[u])
                if u - 2 in s_c:
                    S.wait("vector", "S", s_c[u - 2])
                S.op("vector", lambda e, pi=pi: e.tensor_mul(
                    eprb[:, 160 * pi:160 * pi + 160], ebuf[:, 160 * pi:160 * pi + 160], wcolt[:]))
                vr = S.op("vector", lambda e, pi=pi: e.reduce_sum(
                    small[:, 2 + pi:3 + pi], eprb[:, 160 * pi:160 * pi + 160], axis=AX), inc="V")
                S.wait_force("vector", "V", vr)
                v_es[u] = S.op("vector", lambda e, pi=pi: e.reciprocal(
                    small[:, 4 + pi:5 + pi], small[:, 2 + pi:3 + pi]), inc="V")

            def st_C(u):
                pi = u % 2
                S.wait("scalar", "V", v_es[u])
                if u - 2 in t_tr:
                    S.wait("scalar", "T", t_tr[u - 2])
                s_c[u] = S.op("scalar", lambda e, pi=pi: e.activation(
                    cbuf[:, 160 * pi:160 * pi + 160], ebuf[:, 160 * pi:160 * pi + 160], AF.Identity,
                    scale=small[:, 4 + pi:5 + pi]), inc="S")

            def st_tr(u):
                p, g = divmod(u, NGRP)
                pi, sl = u % 2, u % 6
                kw0 = CACHE_W * p + 64 * g
                psT = ps[2 + pi]
                if kv_gates is not None:
                    S.wait("tensor", *kv_gates[16 + p])
                S.wait("tensor", "S", s_c[u])
                if u - 2 in s_vt:
                    S.wait("tensor", "S", s_vt[u - 2])
                S.op("tensor", lambda e, pi=pi, psT=psT: e.transpose(
                    psT[:, 0:128], cbuf[:, 160 * pi:160 * pi + 128], ident[:]))
                S.op("tensor", lambda e, pi=pi, psT=psT: e.transpose(
                    psT[0:32, 128:256], cbuf[:, 160 * pi + 128:160 * pi + 160], ident[:]))
                S.op("tensor", lambda e, psT=psT, kw0=kw0: e.transpose(
                    psT[:, 256:384], vcache[:, kw0:kw0 + 128], ident[:]))
                t_tr[u] = S.op("tensor", lambda e, psT=psT, sl=sl: e.transpose(
                    psT[0:32, 384:512], vcomb[:, 32 * sl:32 * sl + 32], ident[:]), inc="T")

            def st_evac(u):
                pi = u % 2
                psT = ps[2 + pi]
                S.wait("scalar", "T", t_tr[u])
                if u - 2 in t_val:
                    S.wait("scalar", "T", t_val[u - 2])
                S.op("scalar", lambda e, pi=pi, psT=psT: e.activation(
                    cts[:, 512 * pi:512 * pi + 128], psT[:, 0:128], AF.Identity))
                S.op("scalar", lambda e, pi=pi, psT=psT: e.activation(
                    cts[0:32, 512 * pi + 128:512 * pi + 256], psT[0:32, 128:256], AF.Identity))
                S.op("scalar", lambda e, pi=pi, psT=psT: e.activation(
                    cts[:, 512 * pi + 256:512 * pi + 384], psT[:, 256:384], AF.Identity))
                s_vt[u] = S.op("scalar", lambda e, pi=pi, psT=psT: e.activation(
                    cts[0:32, 512 * pi + 384:512 * pi + 512], psT[0:32, 384:512], AF.Identity), inc="S")

            def st_value(u):
                pi = u % 2
                psO = ps[4 + pi]
                S.wait("tensor", "S", s_vt[u])
                if u - 2 in s_out:
                    S.wait("tensor", "S", s_out[u - 2])
                S.op("tensor", lambda e, pi=pi, psO=psO: e.matmul(
                    psO[:, 0:128], cts[:, 512 * pi + 256:512 * pi + 384], cts[:, 512 * pi:512 * pi + 128],
                    start=True, stop=False))
                t_val[u] = S.op("tensor", lambda e, pi=pi, psO=psO: e.matmul(
                    psO[:, 0:128], cts[0:32, 512 * pi + 384:512 * pi + 512],
                    cts[0:32, 512 * pi + 128:512 * pi + 256],
                    start=False, stop=True), inc="T")

            def st_out(u):
                p, g = divmod(u, NGRP)
                pi = u % 2
                psO = ps[4 + pi]
                S.wait("scalar", "T", t_val[u])
                S.op("scalar", lambda e, p=p, g=g, psO=psO: e.activation(
                    lnbuf[0:64, 512 * p + 64 * g:512 * p + 64 * g + 64], psO[0:64, 0:64], AF.Identity))
                s_out[u] = S.op("scalar", lambda e, p=p, g=g, psO=psO: e.activation(
                    lnbuf[64:128, 512 * p + 64 * g:512 * p + 64 * g + 64], psO[64:128, 64:128], AF.Identity), inc="S")

            v_stage = {}
            for i in range(8):
                S.wait("tensor", *rel["ps%d" % i])
            v_stage[0] = st_stage(0)
            for n in range(NU + 8):
                if n - 1 >= 0 and n - 1 < NU:
                    st_mask(n - 1)
                if n + 1 < NU:
                    v_stage[n + 1] = st_stage(n + 1)
                if n - 3 >= 0 and n - 3 < NU:
                    st_C(n - 3)
                if n - 1 >= 0 and n - 1 < NU:
                    st_exp(n - 1)
                if n - 2 >= 0 and n - 2 < NU:
                    st_eprc(n - 2)
                if n - 5 >= 0 and n - 5 < NU:
                    st_evac(n - 5)
                if n - 7 >= 0 and n - 7 < NU:
                    st_out(n - 7)
                if n - 4 >= 0 and n - 4 < NU:
                    st_tr(n - 4)
                if n - 6 >= 0 and n - 6 < NU:
                    st_value(n - 6)
                if n < NU:
                    st_scores(n, v_stage[n])
            for i, sem_map in ((0, v_msk), (1, v_msk)):
                rel["ps%d" % i] = ("V", v_msk[NU - 2 + i])
            rel["ps2"] = ("S", s_vt[NU - 2])
            rel["ps3"] = ("S", s_vt[NU - 1])
            rel["ps4"] = ("S", s_out[NU - 2])
            rel["ps5"] = ("S", s_out[NU - 1])
            return t_sc[NU - 1], s_out

        def emit_proj(attn_gates=None):
            s_attn = S.c("S")
            proj_gates = [None] * 8

            def evac(g, p, bank, tval):
                S.wait("vector", "T", tval)
                vval = S.op("vector", lambda e, p=p, bank=bank: e.scalar_tensor_tensor(
                    qbuf[:, 512 * p:512 * p + 512], bank[:, :], bproj[:, p:p + 1],
                    x1buf[:, 512 * p:512 * p + 512], OP.add, OP.add), inc="V")
                proj_gates[p] = ("V", vval)
                rel["ps%d" % p] = ("V", vval)
                return ("V", vval)

            emit_wmm(wp_in, (0,), lnbuf, evac, S.c("V"), src_gates=attn_gates)
            return proj_gates

        t_attn_last = None
        ln1_park = None
        for c in range(-1, NCHUNK):
            warm = (c == -1)
            if _STAGE < 1:
                break
            if not warm:
                emit_ring_shift(t_attn_last)
            S.wait("tensor", "X", x_ready)
            S.wait("vector", "X", x_ready)
            if warm:
                S.wait("vector", "V", v_init_done)
            ffn1_pre = {}
            if ln1_park is not None:
                emit_ln_center2(xbuf, lnbuf, ln1_park, post_k=make_presplit(lnbuf, ffn1_pre))
                ln1_park = None
            else:
                emit_layernorm(xbuf, lnbuf, post_k=make_presplit(lnbuf, ffn1_pre))
            if _STAGE < 2:
                break
            ffn1_gates = emit_ffn(lnbuf, xbuf, x1buf, pre_split=ffn1_pre)
            v_x_dead = S.c("V")
            rel["xbuf"] = ("V", v_x_dead)
            if c < NCHUNK - 1:
                x_ready = emit_x_load(c + 1)
            if _STAGE < 3:
                break
            qkv_pre = {}
            ln2_gates = emit_layernorm(x1buf, lnbuf, src_ready=ffn1_gates,
                                       post_k=make_presplit(lnbuf, qkv_pre))
            if _STAGE < 4:
                break
            kv_gates_box = {}
            ff = None if warm else (lambda: make_pooling_thunks(kv_gates_box))
            kv_gates, leftover = emit_qkv(warm, ln2_gates, qkv_pre,
                                          filler_factory=ff, gates_out=kv_gates_box)
            if _STAGE < 5:
                break
            emit_pooling(kv_gates=kv_gates, thunks=leftover)
            if warm:
                continue
            if _STAGE < 6:
                break
            t_attn_last, attn_souts = emit_attention(kv_gates)
            if _STAGE < 7:
                break
            if c < NCHUNK - 1:
                S.wait("tensor", "X", x_ready)
                S.wait("vector", "X", x_ready)
                ln1_park = emit_ln_stats2(xbuf, None)
            attn_gates = [("S", attn_souts[8 * k + 7]) for k in range(8)]
            proj_gates = emit_proj(attn_gates)
            if _STAGE < 8:
                break
            ffn2_pre = {}
            emit_layernorm(qbuf, lnbuf, src_ready=proj_gates,
                           post_k=make_presplit(lnbuf, ffn2_pre))
            S.wait("vector", "O", rel["x1buf"][1] if rel["x1buf"][0] == "O" else 0)
            emit_ffn(lnbuf, qbuf, x1buf, pre_split=ffn2_pre)
            v_out = S.c("V")
            S.wait("gpsimd", "V", v_out)
            for m in range(8):
                S.op("gpsimd", lambda e, m=m, c=c: e.dma_start(
                    out=y_out[128 * m:128 * m + 128, 512 * c:512 * c + 512],
                    in_=x1buf[:, 512 * m:512 * m + 512]), inc="O", inc_by=16)
            rel["x1buf"] = ("O", S.c("O"))

        if _DUMP:
            S.wait("gpsimd", "T", S.c("T"))
            S.wait("gpsimd", "V", S.c("V"))
            S.wait("gpsimd", "S", S.c("S"))
            for i, buf in enumerate((xbuf, lnbuf, x1buf, qbuf)):
                S.op("gpsimd", lambda e, i=i, buf=buf: e.dma_start(
                    out=dbg_out[:, 4096 * i:4096 * i + 4096], in_=buf[:]), inc="O", inc_by=16)
            S.op("gpsimd", lambda e: e.dma_start(
                out=dbg_out[:, 16384:16384 + 4608], in_=kcache[:]), inc="O", inc_by=16)
            S.op("gpsimd", lambda e: e.dma_start(
                out=dbg_out[:, 20992:20992 + 4608], in_=vcache[:]), inc="O", inc_by=16)
            S.op("gpsimd", lambda e: e.dma_start(
                out=dbg_out[:, 25600:25600 + 1024], in_=wtern[:, 0:1024].bitcast(F32)), inc="O", inc_by=16)
            S.op("gpsimd", lambda e: e.dma_start(
                out=dbg_out[:, 26624 - 16:26624 - 14], in_=dquant[:]), inc="O", inc_by=16)
            S.op("gpsimd", lambda e: e.dma_start(
                out=dbg_out[:, 26624 - 14:26624 - 6], in_=absacc[:]), inc="O", inc_by=16)
        S.wait("gpsimd", "O", S.c("O"))

        @block.tensor
        def _(e):
            for f in S.streams["tensor"]:
                f(e)

        @block.vector
        def _(e):
            for f in S.streams["vector"]:
                f(e)

        @block.scalar
        def _(e):
            for f in S.streams["scalar"]:
                f(e)

        @block.sync
        def _(e):
            for f in S.streams["sync"]:
                f(e)

        @block.gpsimd
        def _(e):
            for f in S.streams["gpsimd"]:
                f(e)

    return nc


def _host_inputs(L0_attn_x, L0_attn_W_attn, L0_attn_b_attn, L0_attn_W_proj,
                 L0_attn_b_proj, L0_ffn_weight, L0_ffn_alpha, L0_ffn_bias):
    x = np.ascontiguousarray(np.asarray(L0_attn_x, dtype=np.float32).reshape(2 * 16384, D).T)
    wq = np.ascontiguousarray(
        np.asarray(L0_attn_W_attn, np.float32).reshape(8, 128, 3, 8, 128).transpose(2, 0, 1, 3, 4))
    wp = np.ascontiguousarray(
        np.asarray(L0_attn_W_proj, np.float32).reshape(8, 128, 1, 8, 128).transpose(2, 0, 1, 3, 4))
    wfT = np.ascontiguousarray(np.asarray(L0_ffn_weight, np.float32).T)
    wf = np.ascontiguousarray(wfT.reshape(8, 128, 8, 128).transpose(2, 1, 0, 3))
    battn = np.ascontiguousarray(np.asarray(L0_attn_b_attn, np.float32).reshape(24, 128).T)
    bproj = np.ascontiguousarray(np.asarray(L0_attn_b_proj, np.float32).reshape(8, 128).T)
    bffn = np.ascontiguousarray(np.asarray(L0_ffn_bias, np.float32).reshape(8, 128).T)
    alpha = np.ascontiguousarray(np.broadcast_to(np.asarray(L0_ffn_alpha, np.float32).reshape(1, 1), (128, 1)))

    amask = np.zeros((128, 160), np.float32)
    for r in range(128):
        i = r % 64
        for j in range(128):
            if not (i + 1 <= j <= i + 64):
                amask[r, j] = NEG
    wcol = np.ones((128, 160), np.float32)
    wcol[:, 128:136] = 8.0
    wcol[:, 136:144] = 16.0
    wcol[:, 144:152] = 32.0
    wcol[:, 152:160] = 64.0
    ident = np.eye(128, dtype=np.float32)

    in_maps = []
    for core in range(NCORES):
        t0 = TOK_CORE * core
        xc = np.zeros((D, TOK_CORE + TC), np.float32)
        xc[:, TC:] = x[:, t0:t0 + TOK_CORE]
        if core % 4 != 0:
            xc[:, 0:TC] = x[:, t0 - TC:t0]
            fl = 1.0
        else:
            fl = 0.0
        flag = np.full((128, 1), fl, np.float32)
        in_maps.append({
            "x": np.ascontiguousarray(xc), "wq": wq, "wp": wp, "wf": wf,
            "battn": battn, "bproj": bproj, "bffn": bffn, "alpha": alpha,
            "flag": flag, "amask": amask, "wcol": wcol, "ident": ident,
        })
    return in_maps


_NC_CACHE = None


def kernel(L0_attn_x, L0_attn_W_attn, L0_attn_b_attn, L0_attn_W_proj,
           L0_attn_b_proj, L0_ffn_weight, L0_ffn_alpha, L0_ffn_bias):
    global _NC_CACHE
    in_maps = _host_inputs(L0_attn_x, L0_attn_W_attn, L0_attn_b_attn, L0_attn_W_proj,
                           L0_attn_b_proj, L0_ffn_weight, L0_ffn_alpha, L0_ffn_bias)
    if _NC_CACHE is None:
        _NC_CACHE = build_nc()
    res = run_bass_kernel_spmd(_NC_CACHE, in_maps, core_ids=list(range(NCORES)))
    ys = [res.results[i]["y"] for i in range(NCORES)]
    full = np.concatenate(ys, axis=1)
    return np.ascontiguousarray(full.T).reshape(2, 16384, D).astype(np.float32)


# version 23
# speedup vs baseline: 1.0459x; 1.0459x over previous
import numpy as np
import concourse.bass as bass
import concourse.mybir as mybir
from concourse.bass_utils import run_bass_kernel_spmd

F32 = mybir.dt.float32
F32R = mybir.dt.float32r
AX = mybir.AxisListType.X
OP = mybir.AluOpType
AF = mybir.ActivationFunctionType

NCORES = 8
D = 1024
TOK_CORE = 4096
TC = 512
NCHUNK = TOK_CORE // TC
KC = D // 128
NPAIR = 8
NGRP = TC // 64
CACHE_W = 576
RING_W = 152
NEG = -1.0e30
_STAGE = 99
_DUMP = False


def _ring_slice(g, lvl):
    if lvl == 1:
        return 8 * g, 8 * g + 8
    if lvl == 2:
        return 72 + 8 * (g // 2), 72 + 8 * (g // 2) + 8
    if lvl == 3:
        return 112 + 8 * (g // 4), 112 + 8 * (g // 4) + 8
    return 136, 144


class Sched:

    ENGS = ("tensor", "vector", "scalar", "sync", "gpsimd")

    def __init__(self, sems):
        self.streams = {e: [] for e in self.ENGS}
        self.sems = sems
        self.counts = {k: 0 for k in sems}
        self.last_wait = {}

    def wait(self, eng, sem, val):
        if val <= 0:
            return
        key = (eng, sem)
        if self.last_wait.get(key, 0) >= val:
            return
        self.last_wait[key] = val
        s = self.sems[sem]
        self.streams[eng].append(lambda e, s=s, v=val: e.wait_ge(s, v))

    def op(self, eng, fn, inc=None, inc_by=1):
        if inc is None:
            self.streams[eng].append(fn)
            return None
        s = self.sems[inc]
        self.counts[inc] += inc_by
        val = self.counts[inc]
        self.streams[eng].append(lambda e, fn=fn, s=s, k=inc_by: fn(e).then_inc(s, k))
        self.last_wait[(eng, inc)] = max(self.last_wait.get((eng, inc), 0), val)
        return val

    def wait_force(self, eng, sem, val):
        s = self.sems[sem]
        self.streams[eng].append(lambda e, s=s, v=val: e.wait_ge(s, v))

    def c(self, sem):
        return self.counts[sem]


def build_nc():
    nc = bass.Bass("TRN2", target_bir_lowering=False, detect_race_conditions=False)

    x_in = nc.dram_tensor("x", [D, TOK_CORE + TC], F32, kind="ExternalInput").ap()
    wq_in = nc.dram_tensor("wq", [3, 8, 128, 8, 128], F32, kind="ExternalInput").ap()
    wp_in = nc.dram_tensor("wp", [1, 8, 128, 8, 128], F32, kind="ExternalInput").ap()
    wf_in = nc.dram_tensor("wf", [8, 128, 8, 128], F32, kind="ExternalInput").ap()
    battn_in = nc.dram_tensor("battn", [128, 24], F32, kind="ExternalInput").ap()
    bproj_in = nc.dram_tensor("bproj", [128, 8], F32, kind="ExternalInput").ap()
    bffn_in = nc.dram_tensor("bffn", [128, 8], F32, kind="ExternalInput").ap()
    alpha_in = nc.dram_tensor("alpha", [128, 1], F32, kind="ExternalInput").ap()
    flag_in = nc.dram_tensor("flag", [128, 1], F32, kind="ExternalInput").ap()
    amask_in = nc.dram_tensor("amask", [128, 160], F32, kind="ExternalInput").ap()
    wcol_in = nc.dram_tensor("wcol", [128, 160], F32, kind="ExternalInput").ap()
    ident_in = nc.dram_tensor("ident", [128, 128], F32, kind="ExternalInput").ap()
    y_out = nc.dram_tensor("y", [D, TOK_CORE], F32, kind="ExternalOutput").ap()
    dbg_out = None
    if _DUMP:
        dbg_out = nc.dram_tensor("dbg", [128, 26624], F32, kind="ExternalOutput").ap()

    import contextlib
    with contextlib.ExitStack() as _stack:
        xbuf = _stack.enter_context(nc.sbuf_tensor("xbuf", [128, 4096], F32))
        lnbuf = _stack.enter_context(nc.sbuf_tensor("lnbuf", [128, 4096], F32))
        x1buf = _stack.enter_context(nc.sbuf_tensor("x1buf", [128, 4096], F32))
        qbuf = _stack.enter_context(nc.sbuf_tensor("qbuf", [128, 4096], F32))
        kcache = _stack.enter_context(nc.sbuf_tensor("kcache", [128, 8 * CACHE_W], F32))
        vcache = _stack.enter_context(nc.sbuf_tensor("vcache", [128, 8 * CACHE_W], F32))
        krings = _stack.enter_context(nc.sbuf_tensor("krings", [128, 8 * RING_W], F32))
        vrings = _stack.enter_context(nc.sbuf_tensor("vrings", [128, 8 * RING_W], F32))
        hibuf = _stack.enter_context(nc.sbuf_tensor("hibuf", [128, 1024], F32R))
        lobuf = _stack.enter_context(nc.sbuf_tensor("lobuf", [128, 1024], F32R))
        sqbuf = _stack.enter_context(nc.sbuf_tensor("sqbuf", [128, 1024], F32))
        wtern = _stack.enter_context(nc.sbuf_tensor("wtern", [128, 8192], F32R))
        panel = _stack.enter_context(nc.sbuf_tensor("panel", [128, 2048], F32))
        whilo = _stack.enter_context(nc.sbuf_tensor("whilo", [128, 4096], F32R))
        battn = _stack.enter_context(nc.sbuf_tensor("sb_battn", [128, 24], F32))
        bproj = _stack.enter_context(nc.sbuf_tensor("sb_bproj", [128, 8], F32))
        bffn = _stack.enter_context(nc.sbuf_tensor("sb_bffn", [128, 8], F32))
        alpha_t = _stack.enter_context(nc.sbuf_tensor("alpha_t", [128, 1], F32))
        flag_t = _stack.enter_context(nc.sbuf_tensor("flag_t", [128, 1], F32))
        amask = _stack.enter_context(nc.sbuf_tensor("sb_amask", [128, 160], F32))
        wcolt = _stack.enter_context(nc.sbuf_tensor("wcolt", [128, 160], F32))
        ident = _stack.enter_context(nc.sbuf_tensor("sb_ident", [128, 128], F32))
        halpha = _stack.enter_context(nc.sbuf_tensor("halpha", [128, 1], F32))
        hbffn = _stack.enter_context(nc.sbuf_tensor("hbffn", [128, 8], F32))
        battn_q = _stack.enter_context(nc.sbuf_tensor("battn_q", [128, 8], F32))
        bkv_fl = _stack.enter_context(nc.sbuf_tensor("bkv_fl", [128, 16], F32))
        ones = _stack.enter_context(nc.sbuf_tensor("ones", [128, 1], F32))
        onesr = _stack.enter_context(nc.sbuf_tensor("onesr", [1, 128], F32))
        stats = _stack.enter_context(nc.sbuf_tensor("stats", [1, 1536], F32))
        dquant = _stack.enter_context(nc.sbuf_tensor("dquant", [128, 2], F32))
        epst = _stack.enter_context(nc.sbuf_tensor("epst", [1, 1], F32))
        epsb = _stack.enter_context(nc.sbuf_tensor("epsb", [128, 1], F32))
        absacc = _stack.enter_context(nc.sbuf_tensor("absacc", [128, 8], F32))
        terntmp = _stack.enter_context(nc.sbuf_tensor("terntmp", [128, 1024], F32))
        bdq = _stack.enter_context(nc.sbuf_tensor("bdq", [128, 384], F32))
        vcomb = _stack.enter_context(nc.sbuf_tensor("vcomb", [128, 192], F32))
        kcomb = _stack.enter_context(nc.sbuf_tensor("kcomb", [128, 128], F32))
        sq8 = _stack.enter_context(nc.sbuf_tensor("sq8", [128, 1024], F32))
        mskS = _stack.enter_context(nc.sbuf_tensor("mskS", [128, 320], F32))
        ebuf = _stack.enter_context(nc.sbuf_tensor("ebuf", [128, 320], F32))
        eprb = _stack.enter_context(nc.sbuf_tensor("eprb", [128, 320], F32))
        cbuf = _stack.enter_context(nc.sbuf_tensor("cbuf", [128, 320], F32))
        small = _stack.enter_context(nc.sbuf_tensor("small", [128, 8], F32))
        cts = _stack.enter_context(nc.sbuf_tensor("cts", [128, 1024], F32))
        ps0 = _stack.enter_context(nc.psum_tensor("ps0", [128, 512], F32))
        ps1 = _stack.enter_context(nc.psum_tensor("ps1", [128, 512], F32))
        ps2 = _stack.enter_context(nc.psum_tensor("ps2", [128, 512], F32))
        ps3 = _stack.enter_context(nc.psum_tensor("ps3", [128, 512], F32))
        ps4 = _stack.enter_context(nc.psum_tensor("ps4", [128, 512], F32))
        ps5 = _stack.enter_context(nc.psum_tensor("ps5", [128, 512], F32))
        ps6 = _stack.enter_context(nc.psum_tensor("ps6", [128, 512], F32))
        ps7 = _stack.enter_context(nc.psum_tensor("ps7", [128, 512], F32))
        semT = _stack.enter_context(nc.semaphore("semT"))
        semV = _stack.enter_context(nc.semaphore("semV"))
        semS = _stack.enter_context(nc.semaphore("semS"))
        semX = _stack.enter_context(nc.semaphore("semX"))
        semC = _stack.enter_context(nc.semaphore("semC"))
        semW0 = _stack.enter_context(nc.semaphore("semW0"))
        semW1 = _stack.enter_context(nc.semaphore("semW1"))
        semO = _stack.enter_context(nc.semaphore("semO"))
        block = _stack.enter_context(nc.Block())
        ps = [ps0, ps1, ps2, ps3, ps4, ps5, ps6, ps7]
        S = Sched({"T": semT, "V": semV, "S": semS, "X": semX, "C": semC,
                   "W0": semW0, "W1": semW1, "O": semO})

        S.op("sync", lambda e: e.dma_start(out=battn[:], in_=battn_in), inc="C", inc_by=16)
        S.op("sync", lambda e: e.dma_start(out=bproj[:], in_=bproj_in), inc="C", inc_by=16)
        S.op("sync", lambda e: e.dma_start(out=bffn[:], in_=bffn_in), inc="C", inc_by=16)
        S.op("sync", lambda e: e.dma_start(out=alpha_t[:], in_=alpha_in), inc="C", inc_by=16)
        S.op("sync", lambda e: e.dma_start(out=flag_t[:], in_=flag_in), inc="C", inc_by=16)
        S.op("sync", lambda e: e.dma_start(out=amask[:], in_=amask_in), inc="C", inc_by=16)
        S.op("sync", lambda e: e.dma_start(out=wcolt[:], in_=wcol_in), inc="C", inc_by=16)
        S.op("sync", lambda e: e.dma_start(out=ident[:], in_=ident_in), inc="C", inc_by=16)
        consts_c = S.c("C")

        S.wait("vector", "C", consts_c)
        S.op("vector", lambda e: e.memset(ones[:], 1.0))
        S.op("vector", lambda e: e.memset(epst[:], 1e-5))
        S.op("vector", lambda e: e.memset(epsb[:], 1e-5))
        if _DUMP:
            for buf in (lnbuf, x1buf, qbuf, kcache, vcache):
                S.op("vector", lambda e, buf=buf: e.memset(buf[:], 0.0))
        S.op("vector", lambda e: e.memset(onesr[:], 1.0))
        S.op("vector", lambda e: e.memset(bdq[:], 0.0))
        S.op("vector", lambda e: e.tensor_scalar_mul(halpha[:], alpha_t[:], 0.5))
        S.op("vector", lambda e: e.tensor_scalar_mul(hbffn[:], bffn[:], 0.5))
        S.op("vector", lambda e: e.tensor_scalar_mul(battn_q[:], battn[:, 0:8], 0.125))
        S.op("vector", lambda e: e.tensor_scalar_mul(bkv_fl[:], battn[:, 8:24], flag_t[:, 0:1]),
             inc="V")

        qrel = [("V", 0), ("V", 0)]
        for p in range(8):
            sl = p % 2
            S.wait("sync", *qrel[sl])
            S.op("sync", lambda e, p=p, sl=sl: e.dma_start(
                out=panel[:, 1024 * sl:1024 * sl + 1024],
                in_=wf_in[p].rearrange("p k m -> p (k m)")), inc="W%d" % sl, inc_by=16)
            S.wait("vector", "W%d" % sl, S.c("W%d" % sl))
            vv = S.op("vector", lambda e, p=p, sl=sl: e.reduce_sum(
                absacc[:, p:p + 1], panel[:, 1024 * sl:1024 * sl + 1024],
                axis=AX, apply_absolute_value=True), inc="V")
            qrel[sl] = ("V", vv)
        v_abs = S.c("V")
        S.wait("tensor", "V", v_abs)
        S.op("tensor", lambda e: e.matmul(ps0[0:1, 0:8], ones[:, 0:1], absacc[:], start=True, stop=True), inc="T")
        S.wait("vector", "T", S.c("T"))
        vv = S.op("vector", lambda e: e.reduce_sum(stats[0:1, 0:1], ps0[0:1, 0:8], axis=AX), inc="V")
        S.wait_force("vector", "V", vv)
        S.op("vector", lambda e: e.tensor_scalar_mul(stats[0:1, 4:5], stats[0:1, 0:1], 0.7 / (1024.0 * 1024.0)),
             inc="V")
        S.wait("tensor", "V", S.c("V"))
        S.op("tensor", lambda e: e.matmul(ps1[:, 0:1], onesr[:], stats[0:1, 4:5], start=True, stop=True), inc="T")
        S.wait("vector", "T", S.c("T"))
        S.op("vector", lambda e: e.tensor_copy(dquant[:, 0:1], ps1[:, 0:1]), inc="V")
        S.wait_force("vector", "V", S.c("V"))
        S.op("vector", lambda e: e.tensor_scalar_mul(dquant[:, 1:2], dquant[:, 0:1], -1.0), inc="V")
        S.wait_force("vector", "V", S.c("V"))
        wtern3 = wtern[:].rearrange("p (k m) -> p k m", k=8)
        for p in range(8):
            sl = p % 2
            S.wait("sync", *qrel[sl])
            S.op("sync", lambda e, p=p, sl=sl: e.dma_start(
                out=panel[:, 1024 * sl:1024 * sl + 1024],
                in_=wf_in[p].rearrange("p k m -> p (k m)")), inc="W%d" % sl, inc_by=16)
            S.wait("vector", "W%d" % sl, S.c("W%d" % sl))
            for h in range(2):
                S.op("vector", lambda e, sl=sl, h=h: e.tensor_single_scalar(
                    terntmp[:, 0:512], panel[:, 1024 * sl + 512 * h:1024 * sl + 512 * h + 512],
                    dquant[:, 0:1], OP.is_gt))
                S.op("vector", lambda e, sl=sl, h=h: e.tensor_single_scalar(
                    terntmp[:, 512:1024], panel[:, 1024 * sl + 512 * h:1024 * sl + 512 * h + 512],
                    dquant[:, 1:2], OP.is_lt))
                S.op("vector", lambda e, p=p, h=h: e.tensor_sub(
                    wtern3[:, 4 * h:4 * h + 4, 128 * p:128 * p + 128],
                    terntmp[:, 0:512].rearrange("p (k m) -> p k m", k=4),
                    terntmp[:, 512:1024].rearrange("p (k m) -> p k m", k=4)), inc="V")
            qrel[sl] = ("V", S.c("V"))
        v_init_done = S.c("V")

        rel = {
            "xbuf": ("V", 0), "x1buf": ("O", 0),
            "pan0": ("V", v_init_done), "pan1": ("V", v_init_done),
            "ps0": ("V", v_init_done), "ps1": ("V", v_init_done),
            "ps2": ("V", 0), "ps3": ("V", 0), "ps4": ("V", 0), "ps5": ("V", 0),
            "ps6": ("V", 0), "ps7": ("V", 0),
            "hilo0": ("T", 0), "hilo1": ("T", 0),
            "whilo0": ("T", 0), "whilo1": ("T", 0),
            "sq0": ("T", 0), "sq1": ("T", 0),
        }
        att_bd = [("T", 0), ("T", 0)]
        att_msk = [("S", 0), ("S", 0)]
        att_cb = [("T", 0), ("T", 0)]

        def emit_x_load(cidx):
            t0 = TC * (cidx + 1)
            sem, val = rel["xbuf"]
            S.wait("sync", sem, val)
            for k in range(KC):
                S.op("sync", lambda e, k=k, t0=t0: e.dma_start(
                    out=xbuf[:, 512 * k:512 * k + 512],
                    in_=x_in[128 * k:128 * k + 128, t0:t0 + 512]), inc="X", inc_by=16)
            return S.c("X")

        x_ready = emit_x_load(-1)

        def emit_layernorm(src, dst, src_ready=None, post_k=None):
            if src_ready is None:
                S.wait("tensor", "V", S.c("V"))
            S.wait("tensor", *rel["ps6"])
            for k in range(KC):
                if src_ready is not None:
                    S.wait("tensor", *src_ready[k])
                S.op("tensor", lambda e, k=k: e.matmul(
                    ps6[0:1, :], ones[:, 0:1], src[:, 512 * k:512 * k + 512],
                    start=(k == 0), stop=(k == KC - 1)))
            S.wait("tensor", *rel["ps7"])
            def sqslot(k):
                base = sq8 if (k % 4) < 2 else terntmp
                return base[:, 512 * (k % 2):512 * (k % 2) + 512]
            sq_t = {}
            t_stats = None
            for k in range(KC):
                if k - 4 in sq_t:
                    S.wait("vector", "T", sq_t[k - 4])
                vv = S.op("vector", lambda e, k=k: e.tensor_mul(
                    sqslot(k), src[:, 512 * k:512 * k + 512],
                    src[:, 512 * k:512 * k + 512]), inc="V")
                S.wait("tensor", "V", vv)
                t_stats = sq_t[k] = S.op("tensor", lambda e, k=k: e.matmul(
                    ps7[0:1, :], ones[:, 0:1], sqslot(k),
                    start=(k == 0), stop=(k == KC - 1)), inc="T")
            S.wait("scalar", "T", t_stats)
            A = sqbuf[:, 0:512]
            B = sqbuf[:, 512:1024]
            sr = {}
            for h in (0, 1):
                c0, c1 = 256 * h, 256 * h + 256
                S.op("scalar", lambda e, c0=c0, c1=c1: e.activation(
                    stats[0:1, c0:c1], ps6[0:1, c0:c1], AF.Identity))
                sr[h] = S.op("scalar", lambda e, c0=c0, c1=c1: e.activation(
                    stats[0:1, 512 + c0:512 + c1], ps7[0:1, c0:c1], AF.Identity), inc="S")
            rel["ps6"] = ("S", sr[1])
            rel["ps7"] = ("S", sr[1])
            banks = {0: ps5, 1: ps4}
            relkey = {0: "ps5", 1: "ps4"}
            tb = {}
            for h in (0, 1):
                c0, c1 = 256 * h, 256 * h + 256
                bank = banks[h]
                S.wait("tensor", "S", sr[h])
                S.wait("tensor", *rel[relkey[h]])
                S.op("tensor", lambda e, c0=c0, c1=c1, bank=bank: e.matmul(
                    bank[:, 0:256], onesr[:], stats[0:1, c0:c1], start=True, stop=False))
                tb[h] = S.op("tensor", lambda e, c0=c0, c1=c1, bank=bank: e.matmul(
                    bank[:, 256:512], onesr[:], stats[0:1, 512 + c0:512 + c1],
                    start=False, stop=True), inc="T")
            sq_f = {}
            for h in (0, 1):
                c0, c1 = 256 * h, 256 * h + 256
                S.wait("scalar", "T", tb[h])
                sq_f[h] = S.op("scalar", lambda e, c0=c0, c1=c1, bank=banks[h]: e.activation(
                    A[:, c0:c1], bank[:, 0:256], AF.Square, scale=1.0 / 1024.0), inc="S")
            vv_f = {}
            for h in (0, 1):
                c0, c1 = 256 * h, 256 * h + 256
                S.wait("vector", "S", sq_f[h])
                vv_f[h] = S.op("vector", lambda e, c0=c0, c1=c1, bank=banks[h]: e.scalar_tensor_tensor(
                    B[:, c0:c1], bank[:, 256:512], 1.0 / 1024.0, A[:, c0:c1],
                    OP.mult, OP.subtract), inc="V")
            ex_f = {}
            for h in (0, 1):
                c0, c1 = 256 * h, 256 * h + 256
                S.wait("scalar", "V", vv_f[h])
                S.op("scalar", lambda e, c0=c0, c1=c1: e.activation(
                    A[:, c0:c1], B[:, c0:c1], AF.Ln, bias=epsb[:, 0:1]))
                ex_f[h] = S.op("scalar", lambda e, c0=c0, c1=c1: e.activation(
                    B[:, c0:c1], A[:, c0:c1], AF.Exp, scale=-0.5), inc="S")
            v_mr = None
            for h in (0, 1):
                c0, c1 = 256 * h, 256 * h + 256
                S.wait("vector", "S", ex_f[h])
                v_mr = S.op("vector", lambda e, c0=c0, c1=c1, bank=banks[h]: e.scalar_tensor_tensor(
                    A[:, c0:c1], bank[:, 0:256], 1.0 / 1024.0, B[:, c0:c1],
                    OP.mult, OP.mult), inc="V")
            rel["ps4"] = ("V", v_mr)
            rel["ps5"] = ("V", v_mr)
            gates = []
            for k in range(KC):
                S.op("vector", lambda e, k=k: e.tensor_mul(dst[:, 512 * k:512 * k + 512],
                                                           src[:, 512 * k:512 * k + 512], B))
                vv = S.op("vector", lambda e, k=k: e.tensor_sub(dst[:, 512 * k:512 * k + 512],
                                                                dst[:, 512 * k:512 * k + 512], A),
                     inc="V")
                gates.append(("V", vv))
                if post_k is not None and k < 2:
                    post_k(k)
            return gates

        def sqslot2(k):
            base = sq8 if (k % 4) < 2 else terntmp
            return base[:, 512 * (k % 2):512 * (k % 2) + 512]

        def emit_ln_stats2(src, src_ready):
            S.wait("tensor", *rel["ps6"])
            for k in range(KC):
                if src_ready is not None:
                    S.wait("tensor", *src_ready[k])
                S.op("tensor", lambda e, k=k: e.matmul(
                    ps6[0:1, :], ones[:, 0:1], src[:, 512 * k:512 * k + 512],
                    start=(k == 0), stop=(k == KC - 1)))
            S.wait("tensor", *rel["ps7"])
            sq_t = {}
            t_stats = None
            for k in range(KC):
                if k - 4 in sq_t:
                    S.wait("vector", "T", sq_t[k - 4])
                vv = S.op("vector", lambda e, k=k: e.tensor_mul(
                    sqslot2(k), src[:, 512 * k:512 * k + 512],
                    src[:, 512 * k:512 * k + 512]), inc="V")
                S.wait("tensor", "V", vv)
                t_stats = sq_t[k] = S.op("tensor", lambda e, k=k: e.matmul(
                    ps7[0:1, :], ones[:, 0:1], sqslot2(k),
                    start=(k == 0), stop=(k == KC - 1)), inc="T")
            S.wait("vector", "T", t_stats)
            S.op("vector", lambda e: e.tensor_scalar_mul(stats[0:1, 0:512], ps6[0:1, :], 1.0 / 1024.0))
            S.op("vector", lambda e: e.tensor_mul(stats[0:1, 512:1024], stats[0:1, 0:512], stats[0:1, 0:512]))
            v_m = S.op("vector", lambda e: e.scalar_tensor_tensor(
                stats[0:1, 1024:1536], ps7[0:1, :], 1.0 / 1024.0, stats[0:1, 512:1024],
                OP.mult, OP.subtract), inc="V")
            rel["ps6"] = ("V", v_m)
            rel["ps7"] = ("V", v_m)
            S.wait("scalar", "V", v_m)
            S.op("scalar", lambda e: e.activation(stats[0:1, 512:1024], stats[0:1, 1024:1536], AF.Ln, bias=epst[0:1, 0:1]))
            s_r = S.op("scalar", lambda e: e.activation(cts[0:1, 0:512], stats[0:1, 512:1024], AF.Exp, scale=-0.5),
                       inc="S")
            S.wait("vector", "S", s_r)
            v_mr = S.op("vector", lambda e: e.tensor_mul(
                cts[0:1, 512:1024], stats[0:1, 0:512], cts[0:1, 0:512]), inc="V")
            return v_mr

        def emit_ln_center2(src, dst, v_mr, post_k=None):
            S.wait("tensor", "V", v_mr)
            S.wait("tensor", *rel["ps6"])
            S.wait("tensor", *rel["ps7"])
            S.op("tensor", lambda e: e.matmul(ps6[:, :], onesr[:], cts[0:1, 0:512], start=True, stop=True))
            t_bc = S.op("tensor", lambda e: e.matmul(ps7[:, :], onesr[:], cts[0:1, 512:1024], start=True, stop=True),
                        inc="T")
            S.wait("vector", "T", t_bc)
            gates = []
            for k in range(KC):
                sq = sqbuf[:, 512 * (k % 2):512 * (k % 2) + 512]
                S.op("vector", lambda e, k=k, sq=sq: e.tensor_mul(sq, src[:, 512 * k:512 * k + 512], ps6[:, :]))
                vv = S.op("vector", lambda e, k=k, sq=sq: e.tensor_sub(dst[:, 512 * k:512 * k + 512], sq, ps7[:, :]),
                     inc="V")
                gates.append(("V", vv))
                if post_k is not None and k < 2:
                    post_k(k)
            v_ln = S.c("V")
            rel["ps6"] = ("V", v_ln)
            rel["ps7"] = ("V", v_ln)
            return gates

        def make_presplit(src, store):
            def post_k(k):
                pr = k % 2
                hi = hibuf[:, 512 * pr:512 * pr + 512]
                lo = lobuf[:, 512 * pr:512 * pr + 512]
                S.wait("vector", *rel["hilo%d" % pr])
                S.op("vector", lambda e, k=k, hi=hi: e.tensor_copy(hi, src[:, 512 * k:512 * k + 512]))
                store[k] = S.op("vector", lambda e, k=k, hi=hi, lo=lo: e.tensor_sub(
                    lo, src[:, 512 * k:512 * k + 512], hi), inc="V")
            return post_k

        def emit_ffn(src, resid, dst, pre_split=None):
            for i in range(8):
                S.wait("tensor", *rel["ps%d" % i])
            bank_t = {}
            for k in range(KC):
                pr = k % 2
                hi = hibuf[:, 512 * pr:512 * pr + 512]
                lo = lobuf[:, 512 * pr:512 * pr + 512]
                if pre_split is not None and k in pre_split:
                    S.wait("tensor", "V", pre_split[k])
                else:
                    S.wait("vector", *rel["hilo%d" % pr])
                    S.op("vector", lambda e, k=k, hi=hi: e.tensor_copy(hi, src[:, 512 * k:512 * k + 512]))
                    S.op("vector", lambda e, k=k, hi=hi, lo=lo: e.tensor_sub(lo, src[:, 512 * k:512 * k + 512], hi),
                         inc="V")
                    S.wait("tensor", "V", S.c("V"))
                last = (k == KC - 1)
                first = (k == 0)
                tval = None
                for m in range(8):
                    wt = wtern[:, 1024 * k + 128 * m:1024 * k + 128 * m + 128]
                    S.op("tensor", lambda e, m=m, wt=wt, hi=hi, first=first: e.matmul(
                        ps[m][:, :], wt, hi, start=first, stop=False))
                    tval = S.op("tensor", lambda e, m=m, wt=wt, lo=lo, last=last: e.matmul(
                        ps[m][:, :], wt, lo, start=False, stop=last),
                        inc="T" if (m == 7 or last) else None)
                    if last:
                        bank_t[m] = tval
                rel["hilo%d" % pr] = ("T", tval)
            t_ffn = S.c("T")
            S.wait("vector", "T", t_ffn)
            sem, val = rel["x1buf"] if dst is x1buf else ("V", 0)
            if val:
                S.wait("vector", sem, val)
            evac_gates = []
            evac_vv = {}
            for m in range(8):
                sq = sqbuf[:, 512 * (m % 2):512 * (m % 2) + 512]
                S.wait("scalar", "T", bank_t[m])
                if m - 2 in evac_vv:
                    S.wait("scalar", "V", evac_vv[m - 2])
                sv = S.op("scalar", lambda e, m=m, sq=sq: e.activation(
                    sq, ps[m][:, :], AF.Identity, bias=hbffn[:, m:m + 1], scale=halpha[:, 0:1]), inc="S")
                S.wait("vector", "S", sv)
                vv = S.op("vector", lambda e, m=m, sq=sq: e.tensor_add(
                    dst[:, 512 * m:512 * m + 512], sq, resid[:, 512 * m:512 * m + 512]), inc="V")
                evac_vv[m] = vv
                evac_gates.append(("V", vv))
            v_ffn = S.c("V")
            for i in range(8):
                rel["ps%d" % i] = ("V", v_ffn)
            rel["sq0"] = ("V", v_ffn)
            rel["sq1"] = ("V", v_ffn)
            return evac_gates

        def emit_wmm(w_in, groups, src, evac_fn, v_src, src_gates=None, pre_split=None,
                     filler_factory=None, gates_out=None):
            gates = gates_out if gates_out is not None else {}
            slab_done = {}

            def issue_slab(g, k, si):
                sl = si % 2
                S.wait("sync", *rel["pan%d" % sl])
                S.op("sync", lambda e, g=g, k=k, sl=sl: e.dma_start(
                    out=panel[:, 1024 * sl:1024 * sl + 1024],
                    in_=w_in[g, k].rearrange("p k m -> p (k m)")), inc="W%d" % sl, inc_by=16)
                slab_done[(g, k)] = S.c("W%d" % sl)

            seq = [(g, k) for g in groups for k in range(KC)]
            issue_slab(*seq[0], 0)
            if len(seq) > 1:
                issue_slab(*seq[1], 1)
            filler = None
            for si, (g, k) in enumerate(seq):
                sl = si % 2
                if filler_factory is not None and g == groups[-1] and k == 0:
                    filler = filler_factory()
                    filler_factory = None
                S.wait("vector", "W%d" % sl, slab_done[(g, k)])
                S.wait("vector", *rel["whilo%d" % sl])
                S.op("vector", lambda e, sl=sl: e.tensor_copy(
                    whilo[:, 2048 * sl:2048 * sl + 1024], panel[:, 1024 * sl:1024 * sl + 1024]))
                v_wsplit = S.op("vector", lambda e, sl=sl: e.tensor_sub(
                    whilo[:, 2048 * sl + 1024:2048 * sl + 2048], panel[:, 1024 * sl:1024 * sl + 1024],
                    whilo[:, 2048 * sl:2048 * sl + 1024]), inc="V")
                pr = k % 2
                hi = hibuf[:, 512 * pr:512 * pr + 512]
                lo = lobuf[:, 512 * pr:512 * pr + 512]
                if pre_split is not None and si < 2 and k in pre_split:
                    if k == 0:
                        for i in range(8):
                            S.wait("tensor", *rel["ps%d" % i])
                    S.wait("tensor", "V", max(pre_split[k], v_wsplit))
                else:
                    S.wait("vector", *rel["hilo%d" % pr])
                    if src_gates is not None:
                        S.wait("vector", *src_gates[k])
                    else:
                        S.wait("vector", "V", v_src)
                    S.op("vector", lambda e, k=k, hi=hi: e.tensor_copy(hi, src[:, 512 * k:512 * k + 512]))
                    S.op("vector", lambda e, k=k, hi=hi, lo=lo: e.tensor_sub(
                        lo, src[:, 512 * k:512 * k + 512], hi), inc="V")
                    if k == 0:
                        for i in range(8):
                            S.wait("tensor", *rel["ps%d" % i])
                    S.wait("tensor", "V", S.c("V"))
                tval = None
                last = (k == KC - 1)
                bank_t = {}
                for p in range(8):
                    whi = whilo[:, 2048 * sl + 128 * p:2048 * sl + 128 * p + 128]
                    wlo = whilo[:, 2048 * sl + 1024 + 128 * p:2048 * sl + 1024 + 128 * p + 128]
                    S.op("tensor", lambda e, p=p, whi=whi, hi=hi, k=k: e.matmul(
                        ps[p][:, :], whi, hi, start=(k == 0), stop=False))
                    S.op("tensor", lambda e, p=p, wlo=wlo, hi=hi: e.matmul(
                        ps[p][:, :], wlo, hi, start=False, stop=False))
                    tval = S.op("tensor", lambda e, p=p, whi=whi, lo=lo, last=last: e.matmul(
                        ps[p][:, :], whi, lo, start=False, stop=last),
                        inc="T" if (p == 7 or last) else None)
                    if last:
                        bank_t[p] = tval
                rel["hilo%d" % pr] = ("T", tval)
                rel["whilo%d" % sl] = ("T", tval)
                rel["pan%d" % sl] = ("V", v_wsplit)
                if filler:
                    for _ in range(min(18, len(filler))):
                        filler.pop(0)()
                if si + 2 < len(seq):
                    issue_slab(*seq[si + 2], si + 2)
                if k == KC - 1:
                    for p in range(8):
                        gates[8 * g + p] = evac_fn(g, p, ps[p], bank_t[p])
            return gates, filler

        def emit_qkv(warm, ln2_gates=None, qkv_pre=None, filler_factory=None, gates_out=None):
            groups = (1, 2) if warm else (1, 2, 0)
            v_ln2 = S.c("V")

            def evac(g, p, bank, tval):
                pid = 8 * g + p
                S.wait("scalar", "T", tval)
                if pid < 8:
                    sval = S.op("scalar", lambda e, pid=pid, bank=bank: e.activation(
                        qbuf[:, 512 * pid:512 * pid + 512], bank[:, :], AF.Identity,
                        bias=battn_q[:, pid:pid + 1], scale=0.125), inc="S")
                else:
                    cache = kcache if pid < 16 else vcache
                    j = (pid - 8) % 8
                    if warm:
                        sval = S.op("scalar", lambda e, bank=bank, cache=cache, j=j, pid=pid: e.activation(
                            cache[:, CACHE_W * j + 64:CACHE_W * j + 576], bank[:, :], AF.Identity,
                            bias=bkv_fl[:, pid - 8:pid - 7], scale=flag_t[:, 0:1]), inc="S")
                    else:
                        sval = S.op("scalar", lambda e, bank=bank, cache=cache, j=j, pid=pid: e.activation(
                            cache[:, CACHE_W * j + 64:CACHE_W * j + 576], bank[:, :], AF.Identity,
                            bias=battn[:, pid:pid + 1]), inc="S")
                rel["ps%d" % p] = ("S", sval)
                return ("S", sval)

            return emit_wmm(wq_in, groups, lnbuf, evac, v_ln2, src_gates=ln2_gates, pre_split=qkv_pre,
                            filler_factory=filler_factory, gates_out=gates_out)

        def make_pooling_thunks(kv_gates=None):
            thunks = []

            def T(fn):
                thunks.append(fn)

            s_qkv = S.c("S")
            combos = [(p, kv) for kv in range(2) for p in range(NPAIR)]
            cach = (kcache, vcache)
            rngs = (krings, vrings)

            def h1ap(i):
                j = i % 8
                base = sq8 if j < 4 else terntmp
                return base[:, 256 * (j % 4):256 * (j % 4) + 256]

            def h2ap(i):
                return cts[:, 128 * (i % 8):128 * (i % 8) + 128]

            for half in (0, 1):
                for i in range(8 * half, 8 * half + 8):
                    p, kv = combos[i]
                    cur = cach[kv][:, CACHE_W * p + 64:CACHE_W * p + 576]

                    def t_h1(i=i, p=p, kv=kv, cur=cur):
                        if kv_gates is not None:
                            S.wait("vector", *kv_gates[(8 if kv == 0 else 16) + p])
                        else:
                            S.wait("vector", "S", s_qkv)
                        S.op("vector", lambda e, cur=cur, i=i: e.tensor_add(
                            h1ap(i), cur[:, 0:512:2], cur[:, 1:512:2]))
                    T(t_h1)
                for i in range(8 * half, 8 * half + 8):
                    T(lambda i=i: S.op("vector", lambda e, i=i: e.tensor_add(
                        h2ap(i), h1ap(i)[:, 0:256:2], h1ap(i)[:, 1:256:2])))
                for i in range(8 * half, 8 * half + 8):
                    p, kv = combos[i]
                    r0 = RING_W * p
                    T(lambda i=i, ring=rngs[kv], r0=r0: S.op("vector", lambda e, i=i, ring=ring, r0=r0: e.tensor_add(
                        ring[:, r0 + 8:r0 + 72], h2ap(i)[:, 0:128:2], h2ap(i)[:, 1:128:2])))
            for i, (p, kv) in enumerate(combos):
                r0 = RING_W * p
                T(lambda ring=rngs[kv], r0=r0: S.op("vector", lambda e, ring=ring, r0=r0: e.tensor_add(
                    ring[:, r0 + 80:r0 + 112], ring[:, r0 + 8:r0 + 72:2], ring[:, r0 + 9:r0 + 72:2])))
            for i, (p, kv) in enumerate(combos):
                r0 = RING_W * p
                T(lambda ring=rngs[kv], r0=r0: S.op("vector", lambda e, ring=ring, r0=r0: e.tensor_add(
                    ring[:, r0 + 120:r0 + 136], ring[:, r0 + 80:r0 + 112:2], ring[:, r0 + 81:r0 + 112:2])))
            for i, (p, kv) in enumerate(combos):
                r0 = RING_W * p
                T(lambda ring=rngs[kv], r0=r0: S.op("vector", lambda e, ring=ring, r0=r0: e.tensor_add(
                    ring[:, r0 + 144:r0 + 152], ring[:, r0 + 120:r0 + 136:2], ring[:, r0 + 121:r0 + 136:2])))
            nsc = [0]
            for sc, lo_, hi_ in ((0.125, 8, 72), (1.0 / 16.0, 80, 112),
                                 (1.0 / 32.0, 120, 136), (1.0 / 64.0, 144, 152)):
                for p in range(NPAIR):
                    r0 = RING_W * p
                    nsc[0] += 1
                    last = (nsc[0] == 32)

                    def t_sc2(r0=r0, sc=sc, lo_=lo_, hi_=hi_, last=last):
                        S.op("vector", lambda e, r0=r0, sc=sc, lo_=lo_, hi_=hi_: e.tensor_scalar_mul(
                            krings[:, r0 + lo_:r0 + hi_], krings[:, r0 + lo_:r0 + hi_], sc),
                            inc="V" if last else None)
                    T(t_sc2)
            return thunks

        def emit_pooling(kv_gates=None, thunks=None):
            if thunks is None:
                thunks = make_pooling_thunks(kv_gates)
            while thunks:
                thunks.pop(0)()
            return S.c("V")

        def emit_ring_shift(t_gate=None):
            S.wait("vector", "T", t_gate if t_gate is not None else S.c("T"))
            for p in range(NPAIR):
                for ring in (krings, vrings):
                    r0 = RING_W * p
                    S.op("vector", lambda e, ring=ring, r0=r0: e.tensor_copy(ring[:, r0 + 0:r0 + 8], ring[:, r0 + 64:r0 + 72]))
                    S.op("vector", lambda e, ring=ring, r0=r0: e.tensor_copy(ring[:, r0 + 72:r0 + 80], ring[:, r0 + 104:r0 + 112]))
                    S.op("vector", lambda e, ring=ring, r0=r0: e.tensor_copy(ring[:, r0 + 112:r0 + 120], ring[:, r0 + 128:r0 + 136]))
                    S.op("vector", lambda e, ring=ring, r0=r0: e.tensor_copy(ring[:, r0 + 136:r0 + 144], ring[:, r0 + 144:r0 + 152]))
                for cache in (kcache, vcache):
                    c0 = CACHE_W * p
                    S.op("vector", lambda e, cache=cache, c0=c0: e.tensor_copy(
                        cache[:, c0:c0 + 64], cache[:, c0 + 512:c0 + 576]))
            return S.op("vector", lambda e: e.memset(small[:, 6:7], 0.0), inc="V")

        def emit_attention(kv_gates=None):
            s_qkv = S.c("S")
            NU = NPAIR * NGRP
            t_sc, v_msk, s_exp, v_es, s_c, t_tr, s_vt, t_val, s_out = ({} for _ in range(9))

            def st_stage(u):
                p, g = divmod(u, NGRP)
                sl = u % 3
                q0 = 512 * p + 64 * g
                sl8 = u % 6
                if kv_gates is not None:
                    S.wait("vector", *kv_gates[p])
                else:
                    S.wait("vector", "S", s_qkv)
                if u - 3 in t_sc:
                    S.wait("vector", "T", t_sc[u - 3])
                if u - 6 in t_tr:
                    S.wait("vector", "T", t_tr[u - 6])
                S.op("vector", lambda e, q0=q0, sl=sl: e.tensor_copy(
                    bdq[0:64, 128 * sl:128 * sl + 64], qbuf[0:64, q0:q0 + 64]))
                S.op("vector", lambda e, q0=q0, sl=sl: e.tensor_copy(
                    bdq[64:128, 128 * sl + 64:128 * sl + 128], qbuf[64:128, q0:q0 + 64]))
                for j, lvl in enumerate((1, 2, 3, 4)):
                    a, b = _ring_slice(g, lvl)
                    S.op("vector", lambda e, p=p, a=a, b=b, j=j, sl8=sl8: e.tensor_copy(
                        vcomb[:, 32 * sl8 + 8 * j:32 * sl8 + 8 * j + 8],
                        vrings[:, RING_W * p + a:RING_W * p + b]))
                vs = None
                for j, lvl in enumerate((1, 2, 3, 4)):
                    a, b = _ring_slice(g, lvl)
                    vs = S.op("vector", lambda e, p=p, a=a, b=b, j=j, sl=sl: e.tensor_copy(
                        kcomb[:, 32 * sl + 8 * j:32 * sl + 8 * j + 8],
                        krings[:, RING_W * p + a:RING_W * p + b]), inc="V" if j == 3 else None)
                return vs

            def psS_ap(u):
                return ps[u % 2][:, 256 * ((u // 2) % 2):256 * ((u // 2) % 2) + 160]

            def st_scores(u, v_stage):
                p, g = divmod(u, NGRP)
                sl = u % 3
                kw0 = CACHE_W * p + 64 * g
                if kv_gates is not None:
                    S.wait("tensor", *kv_gates[8 + p])
                S.wait("tensor", "V", v_stage)
                if u - 4 in v_msk:
                    S.wait("tensor", "V", v_msk[u - 4])
                pS = psS_ap(u)
                S.op("tensor", lambda e, sl=sl, pS=pS, kw0=kw0: e.matmul(
                    pS[:, 0:128], bdq[:, 128 * sl:128 * sl + 128], kcache[:, kw0:kw0 + 128],
                    start=True, stop=False))
                t_sc[u] = S.op("tensor", lambda e, sl=sl, pS=pS: e.matmul(
                    pS[:, 128:160], bdq[:, 128 * sl:128 * sl + 128],
                    kcomb[:, 32 * sl:32 * sl + 32],
                    start=False, stop=True), inc="T")

            def st_mask(u):
                pi = u % 2
                S.wait("vector", "T", t_sc[u])
                if u - 2 in s_exp:
                    S.wait("vector", "S", s_exp[u - 2])
                pS = psS_ap(u)
                S.op("vector", lambda e, pi=pi, pS=pS: e.tensor_add(
                    mskS[:, 160 * pi:160 * pi + 160], pS[:, 0:160], amask[:]))
                v_msk[u] = S.op("vector", lambda e, pi=pi: e.reduce_max(
                    small[:, pi:pi + 1], mskS[:, 160 * pi:160 * pi + 160], axis=AX, negate=True), inc="V")

            def st_exp(u):
                pi = u % 2
                S.wait("scalar", "V", v_msk[u])
                s_exp[u] = S.op("scalar", lambda e, pi=pi: e.activation(
                    ebuf[:, 160 * pi:160 * pi + 160], mskS[:, 160 * pi:160 * pi + 160], AF.Exp,
                    bias=small[:, pi:pi + 1]), inc="S")

            def st_eprc(u):
                pi = u % 2
                S.wait("vector", "S", s_exp[u])
                if u - 2 in s_c:
                    S.wait("vector", "S", s_c[u - 2])
                S.op("vector", lambda e, pi=pi: e.tensor_mul(
                    eprb[:, 160 * pi:160 * pi + 160], ebuf[:, 160 * pi:160 * pi + 160], wcolt[:]))
                vr = S.op("vector", lambda e, pi=pi: e.reduce_sum(
                    small[:, 2 + pi:3 + pi], eprb[:, 160 * pi:160 * pi + 160], axis=AX), inc="V")
                S.wait_force("vector", "V", vr)
                v_es[u] = S.op("vector", lambda e, pi=pi: e.reciprocal(
                    small[:, 4 + pi:5 + pi], small[:, 2 + pi:3 + pi]), inc="V")

            def st_C(u):
                pi = u % 2
                S.wait("scalar", "V", v_es[u])
                if u - 2 in t_tr:
                    S.wait("scalar", "T", t_tr[u - 2])
                s_c[u] = S.op("scalar", lambda e, pi=pi: e.activation(
                    cbuf[:, 160 * pi:160 * pi + 160], ebuf[:, 160 * pi:160 * pi + 160], AF.Identity,
                    scale=small[:, 4 + pi:5 + pi]), inc="S")

            def st_tr(u):
                p, g = divmod(u, NGRP)
                pi, sl = u % 2, u % 6
                kw0 = CACHE_W * p + 64 * g
                psT = ps[2 + pi]
                if kv_gates is not None:
                    S.wait("tensor", *kv_gates[16 + p])
                S.wait("tensor", "S", s_c[u])
                if u - 2 in s_vt:
                    S.wait("tensor", "S", s_vt[u - 2])
                S.op("tensor", lambda e, pi=pi, psT=psT: e.transpose(
                    psT[:, 0:128], cbuf[:, 160 * pi:160 * pi + 128], ident[:]))
                S.op("tensor", lambda e, pi=pi, psT=psT: e.transpose(
                    psT[0:32, 128:256], cbuf[:, 160 * pi + 128:160 * pi + 160], ident[:]))
                S.op("tensor", lambda e, psT=psT, kw0=kw0: e.transpose(
                    psT[:, 256:384], vcache[:, kw0:kw0 + 128], ident[:]))
                t_tr[u] = S.op("tensor", lambda e, psT=psT, sl=sl: e.transpose(
                    psT[0:32, 384:512], vcomb[:, 32 * sl:32 * sl + 32], ident[:]), inc="T")

            def st_evac(u):
                pi = u % 2
                psT = ps[2 + pi]
                S.wait("scalar", "T", t_tr[u])
                if u - 2 in t_val:
                    S.wait("scalar", "T", t_val[u - 2])
                S.op("scalar", lambda e, pi=pi, psT=psT: e.activation(
                    cts[:, 512 * pi:512 * pi + 128], psT[:, 0:128], AF.Identity))
                S.op("scalar", lambda e, pi=pi, psT=psT: e.activation(
                    cts[0:32, 512 * pi + 128:512 * pi + 256], psT[0:32, 128:256], AF.Identity))
                S.op("scalar", lambda e, pi=pi, psT=psT: e.activation(
                    cts[:, 512 * pi + 256:512 * pi + 384], psT[:, 256:384], AF.Identity))
                s_vt[u] = S.op("scalar", lambda e, pi=pi, psT=psT: e.activation(
                    cts[0:32, 512 * pi + 384:512 * pi + 512], psT[0:32, 384:512], AF.Identity), inc="S")

            def st_value(u):
                pi = u % 2
                psO = ps[4 + pi]
                S.wait("tensor", "S", s_vt[u])
                if u - 2 in s_out:
                    S.wait("tensor", "S", s_out[u - 2])
                S.op("tensor", lambda e, pi=pi, psO=psO: e.matmul(
                    psO[:, 0:128], cts[:, 512 * pi + 256:512 * pi + 384], cts[:, 512 * pi:512 * pi + 128],
                    start=True, stop=False))
                t_val[u] = S.op("tensor", lambda e, pi=pi, psO=psO: e.matmul(
                    psO[:, 0:128], cts[0:32, 512 * pi + 384:512 * pi + 512],
                    cts[0:32, 512 * pi + 128:512 * pi + 256],
                    start=False, stop=True), inc="T")

            def st_out(u):
                p, g = divmod(u, NGRP)
                pi = u % 2
                psO = ps[4 + pi]
                S.wait("scalar", "T", t_val[u])
                S.op("scalar", lambda e, p=p, g=g, psO=psO: e.activation(
                    lnbuf[0:64, 512 * p + 64 * g:512 * p + 64 * g + 64], psO[0:64, 0:64], AF.Identity))
                s_out[u] = S.op("scalar", lambda e, p=p, g=g, psO=psO: e.activation(
                    lnbuf[64:128, 512 * p + 64 * g:512 * p + 64 * g + 64], psO[64:128, 64:128], AF.Identity), inc="S")

            v_stage = {}
            for i in range(8):
                S.wait("tensor", *rel["ps%d" % i])
            v_stage[0] = st_stage(0)
            for n in range(NU + 8):
                if n - 1 >= 0 and n - 1 < NU:
                    st_mask(n - 1)
                if n + 1 < NU:
                    v_stage[n + 1] = st_stage(n + 1)
                if n - 3 >= 0 and n - 3 < NU:
                    st_C(n - 3)
                if n - 1 >= 0 and n - 1 < NU:
                    st_exp(n - 1)
                if n - 2 >= 0 and n - 2 < NU:
                    st_eprc(n - 2)
                if n - 5 >= 0 and n - 5 < NU:
                    st_evac(n - 5)
                if n - 7 >= 0 and n - 7 < NU:
                    st_out(n - 7)
                if n - 4 >= 0 and n - 4 < NU:
                    st_tr(n - 4)
                if n - 6 >= 0 and n - 6 < NU:
                    st_value(n - 6)
                if n < NU:
                    st_scores(n, v_stage[n])
            for i, sem_map in ((0, v_msk), (1, v_msk)):
                rel["ps%d" % i] = ("V", v_msk[NU - 2 + i])
            rel["ps2"] = ("S", s_vt[NU - 2])
            rel["ps3"] = ("S", s_vt[NU - 1])
            rel["ps4"] = ("S", s_out[NU - 2])
            rel["ps5"] = ("S", s_out[NU - 1])
            return t_sc[NU - 1], s_out

        def emit_proj(attn_gates=None):
            s_attn = S.c("S")
            proj_gates = [None] * 8

            def evac(g, p, bank, tval):
                S.wait("vector", "T", tval)
                vval = S.op("vector", lambda e, p=p, bank=bank: e.scalar_tensor_tensor(
                    qbuf[:, 512 * p:512 * p + 512], bank[:, :], bproj[:, p:p + 1],
                    x1buf[:, 512 * p:512 * p + 512], OP.add, OP.add), inc="V")
                proj_gates[p] = ("V", vval)
                rel["ps%d" % p] = ("V", vval)
                return ("V", vval)

            emit_wmm(wp_in, (0,), lnbuf, evac, S.c("V"), src_gates=attn_gates)
            return proj_gates

        t_attn_last = None
        ln1_park = None
        for c in range(-1, NCHUNK):
            warm = (c == -1)
            if _STAGE < 1:
                break
            if not warm:
                emit_ring_shift(t_attn_last)
            S.wait("tensor", "X", x_ready)
            S.wait("vector", "X", x_ready)
            if warm:
                S.wait("vector", "V", v_init_done)
            ffn1_pre = {}
            if ln1_park is not None:
                emit_ln_center2(xbuf, lnbuf, ln1_park, post_k=make_presplit(lnbuf, ffn1_pre))
                ln1_park = None
            else:
                emit_layernorm(xbuf, lnbuf, post_k=make_presplit(lnbuf, ffn1_pre))
            if _STAGE < 2:
                break
            ffn1_gates = emit_ffn(lnbuf, xbuf, x1buf, pre_split=ffn1_pre)
            v_x_dead = S.c("V")
            rel["xbuf"] = ("V", v_x_dead)
            if c < NCHUNK - 1:
                x_ready = emit_x_load(c + 1)
            if _STAGE < 3:
                break
            qkv_pre = {}
            ln2_gates = emit_layernorm(x1buf, lnbuf, src_ready=ffn1_gates,
                                       post_k=make_presplit(lnbuf, qkv_pre))
            if _STAGE < 4:
                break
            kv_gates_box = {}
            ff = None if warm else (lambda: make_pooling_thunks(kv_gates_box))
            kv_gates, leftover = emit_qkv(warm, ln2_gates, qkv_pre,
                                          filler_factory=ff, gates_out=kv_gates_box)
            if _STAGE < 5:
                break
            emit_pooling(kv_gates=kv_gates, thunks=leftover)
            if warm:
                continue
            if _STAGE < 6:
                break
            t_attn_last, attn_souts = emit_attention(kv_gates)
            if _STAGE < 7:
                break
            if c < NCHUNK - 1:
                S.wait("tensor", "X", x_ready)
                S.wait("vector", "X", x_ready)
                ln1_park = emit_ln_stats2(xbuf, None)
            attn_gates = [("S", attn_souts[8 * k + 7]) for k in range(8)]
            proj_gates = emit_proj(attn_gates)
            if _STAGE < 8:
                break
            ffn2_pre = {}
            emit_layernorm(qbuf, lnbuf, src_ready=proj_gates,
                           post_k=make_presplit(lnbuf, ffn2_pre))
            S.wait("vector", "O", rel["x1buf"][1] if rel["x1buf"][0] == "O" else 0)
            emit_ffn(lnbuf, qbuf, x1buf, pre_split=ffn2_pre)
            v_out = S.c("V")
            S.wait("gpsimd", "V", v_out)
            for m in range(8):
                S.op("gpsimd", lambda e, m=m, c=c: e.dma_start(
                    out=y_out[128 * m:128 * m + 128, 512 * c:512 * c + 512],
                    in_=x1buf[:, 512 * m:512 * m + 512]), inc="O", inc_by=16)
            rel["x1buf"] = ("O", S.c("O"))

        if _DUMP:
            S.wait("gpsimd", "T", S.c("T"))
            S.wait("gpsimd", "V", S.c("V"))
            S.wait("gpsimd", "S", S.c("S"))
            for i, buf in enumerate((xbuf, lnbuf, x1buf, qbuf)):
                S.op("gpsimd", lambda e, i=i, buf=buf: e.dma_start(
                    out=dbg_out[:, 4096 * i:4096 * i + 4096], in_=buf[:]), inc="O", inc_by=16)
            S.op("gpsimd", lambda e: e.dma_start(
                out=dbg_out[:, 16384:16384 + 4608], in_=kcache[:]), inc="O", inc_by=16)
            S.op("gpsimd", lambda e: e.dma_start(
                out=dbg_out[:, 20992:20992 + 4608], in_=vcache[:]), inc="O", inc_by=16)
            S.op("gpsimd", lambda e: e.dma_start(
                out=dbg_out[:, 25600:25600 + 1024], in_=wtern[:, 0:1024].bitcast(F32)), inc="O", inc_by=16)
            S.op("gpsimd", lambda e: e.dma_start(
                out=dbg_out[:, 26624 - 16:26624 - 14], in_=dquant[:]), inc="O", inc_by=16)
            S.op("gpsimd", lambda e: e.dma_start(
                out=dbg_out[:, 26624 - 14:26624 - 6], in_=absacc[:]), inc="O", inc_by=16)
        S.wait("gpsimd", "O", S.c("O"))

        @block.tensor
        def _(e):
            for f in S.streams["tensor"]:
                f(e)

        @block.vector
        def _(e):
            for f in S.streams["vector"]:
                f(e)

        @block.scalar
        def _(e):
            for f in S.streams["scalar"]:
                f(e)

        @block.sync
        def _(e):
            for f in S.streams["sync"]:
                f(e)

        @block.gpsimd
        def _(e):
            for f in S.streams["gpsimd"]:
                f(e)

    return nc


def _host_inputs(L0_attn_x, L0_attn_W_attn, L0_attn_b_attn, L0_attn_W_proj,
                 L0_attn_b_proj, L0_ffn_weight, L0_ffn_alpha, L0_ffn_bias):
    x = np.ascontiguousarray(np.asarray(L0_attn_x, dtype=np.float32).reshape(2 * 16384, D).T)
    wq = np.ascontiguousarray(
        np.asarray(L0_attn_W_attn, np.float32).reshape(8, 128, 3, 8, 128).transpose(2, 0, 1, 3, 4))
    wp = np.ascontiguousarray(
        np.asarray(L0_attn_W_proj, np.float32).reshape(8, 128, 1, 8, 128).transpose(2, 0, 1, 3, 4))
    wfT = np.ascontiguousarray(np.asarray(L0_ffn_weight, np.float32).T)
    wf = np.ascontiguousarray(wfT.reshape(8, 128, 8, 128).transpose(2, 1, 0, 3))
    battn = np.ascontiguousarray(np.asarray(L0_attn_b_attn, np.float32).reshape(24, 128).T)
    bproj = np.ascontiguousarray(np.asarray(L0_attn_b_proj, np.float32).reshape(8, 128).T)
    bffn = np.ascontiguousarray(np.asarray(L0_ffn_bias, np.float32).reshape(8, 128).T)
    alpha = np.ascontiguousarray(np.broadcast_to(np.asarray(L0_ffn_alpha, np.float32).reshape(1, 1), (128, 1)))

    amask = np.zeros((128, 160), np.float32)
    for r in range(128):
        i = r % 64
        for j in range(128):
            if not (i + 1 <= j <= i + 64):
                amask[r, j] = NEG
    wcol = np.ones((128, 160), np.float32)
    wcol[:, 128:136] = 8.0
    wcol[:, 136:144] = 16.0
    wcol[:, 144:152] = 32.0
    wcol[:, 152:160] = 64.0
    ident = np.eye(128, dtype=np.float32)

    in_maps = []
    for core in range(NCORES):
        t0 = TOK_CORE * core
        xc = np.zeros((D, TOK_CORE + TC), np.float32)
        xc[:, TC:] = x[:, t0:t0 + TOK_CORE]
        if core % 4 != 0:
            xc[:, 0:TC] = x[:, t0 - TC:t0]
            fl = 1.0
        else:
            fl = 0.0
        flag = np.full((128, 1), fl, np.float32)
        in_maps.append({
            "x": np.ascontiguousarray(xc), "wq": wq, "wp": wp, "wf": wf,
            "battn": battn, "bproj": bproj, "bffn": bffn, "alpha": alpha,
            "flag": flag, "amask": amask, "wcol": wcol, "ident": ident,
        })
    return in_maps


_NC_CACHE = None


def kernel(L0_attn_x, L0_attn_W_attn, L0_attn_b_attn, L0_attn_W_proj,
           L0_attn_b_proj, L0_ffn_weight, L0_ffn_alpha, L0_ffn_bias):
    global _NC_CACHE
    in_maps = _host_inputs(L0_attn_x, L0_attn_W_attn, L0_attn_b_attn, L0_attn_W_proj,
                           L0_attn_b_proj, L0_ffn_weight, L0_ffn_alpha, L0_ffn_bias)
    if _NC_CACHE is None:
        _NC_CACHE = build_nc()
    res = run_bass_kernel_spmd(_NC_CACHE, in_maps, core_ids=list(range(NCORES)))
    ys = [res.results[i]["y"] for i in range(NCORES)]
    full = np.concatenate(ys, axis=1)
    return np.ascontiguousarray(full.T).reshape(2, 16384, D).astype(np.float32)


# version 24
# speedup vs baseline: 1.0463x; 1.0004x over previous
import numpy as np
import concourse.bass as bass
import concourse.mybir as mybir
from concourse.bass_utils import run_bass_kernel_spmd

F32 = mybir.dt.float32
F32R = mybir.dt.float32r
AX = mybir.AxisListType.X
OP = mybir.AluOpType
AF = mybir.ActivationFunctionType

NCORES = 8
D = 1024
TOK_CORE = 4096
TC = 512
NCHUNK = TOK_CORE // TC
KC = D // 128
NPAIR = 8
NGRP = TC // 64
CACHE_W = 576
RING_W = 152
NEG = -1.0e30
_STAGE = 99
_DUMP = False


def _ring_slice(g, lvl):
    if lvl == 1:
        return 8 * g, 8 * g + 8
    if lvl == 2:
        return 72 + 8 * (g // 2), 72 + 8 * (g // 2) + 8
    if lvl == 3:
        return 112 + 8 * (g // 4), 112 + 8 * (g // 4) + 8
    return 136, 144


class Sched:

    ENGS = ("tensor", "vector", "scalar", "sync", "gpsimd")

    def __init__(self, sems):
        self.streams = {e: [] for e in self.ENGS}
        self.sems = sems
        self.counts = {k: 0 for k in sems}
        self.last_wait = {}

    def wait(self, eng, sem, val):
        if val <= 0:
            return
        key = (eng, sem)
        if self.last_wait.get(key, 0) >= val:
            return
        self.last_wait[key] = val
        s = self.sems[sem]
        self.streams[eng].append(lambda e, s=s, v=val: e.wait_ge(s, v))

    def op(self, eng, fn, inc=None, inc_by=1):
        if inc is None:
            self.streams[eng].append(fn)
            return None
        s = self.sems[inc]
        self.counts[inc] += inc_by
        val = self.counts[inc]
        self.streams[eng].append(lambda e, fn=fn, s=s, k=inc_by: fn(e).then_inc(s, k))
        self.last_wait[(eng, inc)] = max(self.last_wait.get((eng, inc), 0), val)
        return val

    def wait_force(self, eng, sem, val):
        s = self.sems[sem]
        self.streams[eng].append(lambda e, s=s, v=val: e.wait_ge(s, v))

    def c(self, sem):
        return self.counts[sem]


def build_nc():
    nc = bass.Bass("TRN2", target_bir_lowering=False, detect_race_conditions=False)

    x_in = nc.dram_tensor("x", [D, TOK_CORE + TC], F32, kind="ExternalInput").ap()
    wq_in = nc.dram_tensor("wq", [3, 8, 128, 8, 128], F32, kind="ExternalInput").ap()
    wp_in = nc.dram_tensor("wp", [1, 8, 128, 8, 128], F32, kind="ExternalInput").ap()
    wf_in = nc.dram_tensor("wf", [8, 128, 8, 128], F32, kind="ExternalInput").ap()
    battn_in = nc.dram_tensor("battn", [128, 24], F32, kind="ExternalInput").ap()
    bproj_in = nc.dram_tensor("bproj", [128, 8], F32, kind="ExternalInput").ap()
    bffn_in = nc.dram_tensor("bffn", [128, 8], F32, kind="ExternalInput").ap()
    alpha_in = nc.dram_tensor("alpha", [128, 1], F32, kind="ExternalInput").ap()
    flag_in = nc.dram_tensor("flag", [128, 1], F32, kind="ExternalInput").ap()
    amask_in = nc.dram_tensor("amask", [128, 160], F32, kind="ExternalInput").ap()
    wcol_in = nc.dram_tensor("wcol", [128, 160], F32, kind="ExternalInput").ap()
    ident_in = nc.dram_tensor("ident", [128, 128], F32, kind="ExternalInput").ap()
    y_out = nc.dram_tensor("y", [D, TOK_CORE], F32, kind="ExternalOutput").ap()
    dbg_out = None
    if _DUMP:
        dbg_out = nc.dram_tensor("dbg", [128, 26624], F32, kind="ExternalOutput").ap()

    import contextlib
    with contextlib.ExitStack() as _stack:
        xbuf = _stack.enter_context(nc.sbuf_tensor("xbuf", [128, 4096], F32))
        lnbuf = _stack.enter_context(nc.sbuf_tensor("lnbuf", [128, 4096], F32))
        x1buf = _stack.enter_context(nc.sbuf_tensor("x1buf", [128, 4096], F32))
        qbuf = _stack.enter_context(nc.sbuf_tensor("qbuf", [128, 4096], F32))
        kcache = _stack.enter_context(nc.sbuf_tensor("kcache", [128, 8 * CACHE_W], F32))
        vcache = _stack.enter_context(nc.sbuf_tensor("vcache", [128, 8 * CACHE_W], F32))
        krings = _stack.enter_context(nc.sbuf_tensor("krings", [128, 8 * RING_W], F32))
        vrings = _stack.enter_context(nc.sbuf_tensor("vrings", [128, 8 * RING_W], F32))
        hibuf = _stack.enter_context(nc.sbuf_tensor("hibuf", [128, 1024], F32R))
        lobuf = _stack.enter_context(nc.sbuf_tensor("lobuf", [128, 1024], F32R))
        sqbuf = _stack.enter_context(nc.sbuf_tensor("sqbuf", [128, 1024], F32))
        wtern = _stack.enter_context(nc.sbuf_tensor("wtern", [128, 8192], F32R))
        panel = _stack.enter_context(nc.sbuf_tensor("panel", [128, 2048], F32))
        whilo = _stack.enter_context(nc.sbuf_tensor("whilo", [128, 4096], F32R))
        battn = _stack.enter_context(nc.sbuf_tensor("sb_battn", [128, 24], F32))
        bproj = _stack.enter_context(nc.sbuf_tensor("sb_bproj", [128, 8], F32))
        bffn = _stack.enter_context(nc.sbuf_tensor("sb_bffn", [128, 8], F32))
        alpha_t = _stack.enter_context(nc.sbuf_tensor("alpha_t", [128, 1], F32))
        flag_t = _stack.enter_context(nc.sbuf_tensor("flag_t", [128, 1], F32))
        amask = _stack.enter_context(nc.sbuf_tensor("sb_amask", [128, 160], F32))
        wcolt = _stack.enter_context(nc.sbuf_tensor("wcolt", [128, 160], F32))
        ident = _stack.enter_context(nc.sbuf_tensor("sb_ident", [128, 128], F32))
        halpha = _stack.enter_context(nc.sbuf_tensor("halpha", [128, 1], F32))
        hbffn = _stack.enter_context(nc.sbuf_tensor("hbffn", [128, 8], F32))
        battn_q = _stack.enter_context(nc.sbuf_tensor("battn_q", [128, 8], F32))
        bkv_fl = _stack.enter_context(nc.sbuf_tensor("bkv_fl", [128, 16], F32))
        ones = _stack.enter_context(nc.sbuf_tensor("ones", [128, 1], F32))
        onesr = _stack.enter_context(nc.sbuf_tensor("onesr", [1, 128], F32))
        stats = _stack.enter_context(nc.sbuf_tensor("stats", [1, 1536], F32))
        dquant = _stack.enter_context(nc.sbuf_tensor("dquant", [128, 2], F32))
        epst = _stack.enter_context(nc.sbuf_tensor("epst", [1, 1], F32))
        epsb = _stack.enter_context(nc.sbuf_tensor("epsb", [128, 1], F32))
        absacc = _stack.enter_context(nc.sbuf_tensor("absacc", [128, 8], F32))
        terntmp = _stack.enter_context(nc.sbuf_tensor("terntmp", [128, 1024], F32))
        bdq = _stack.enter_context(nc.sbuf_tensor("bdq", [128, 384], F32))
        vcomb = _stack.enter_context(nc.sbuf_tensor("vcomb", [128, 192], F32))
        kcomb = _stack.enter_context(nc.sbuf_tensor("kcomb", [128, 128], F32))
        sq8 = _stack.enter_context(nc.sbuf_tensor("sq8", [128, 1024], F32))
        mskS = _stack.enter_context(nc.sbuf_tensor("mskS", [128, 320], F32))
        ebuf = _stack.enter_context(nc.sbuf_tensor("ebuf", [128, 320], F32))
        eprb = _stack.enter_context(nc.sbuf_tensor("eprb", [128, 320], F32))
        cbuf = _stack.enter_context(nc.sbuf_tensor("cbuf", [128, 320], F32))
        small = _stack.enter_context(nc.sbuf_tensor("small", [128, 8], F32))
        cts = _stack.enter_context(nc.sbuf_tensor("cts", [128, 1024], F32))
        ps0 = _stack.enter_context(nc.psum_tensor("ps0", [128, 512], F32))
        ps1 = _stack.enter_context(nc.psum_tensor("ps1", [128, 512], F32))
        ps2 = _stack.enter_context(nc.psum_tensor("ps2", [128, 512], F32))
        ps3 = _stack.enter_context(nc.psum_tensor("ps3", [128, 512], F32))
        ps4 = _stack.enter_context(nc.psum_tensor("ps4", [128, 512], F32))
        ps5 = _stack.enter_context(nc.psum_tensor("ps5", [128, 512], F32))
        ps6 = _stack.enter_context(nc.psum_tensor("ps6", [128, 512], F32))
        ps7 = _stack.enter_context(nc.psum_tensor("ps7", [128, 512], F32))
        semT = _stack.enter_context(nc.semaphore("semT"))
        semV = _stack.enter_context(nc.semaphore("semV"))
        semS = _stack.enter_context(nc.semaphore("semS"))
        semX = _stack.enter_context(nc.semaphore("semX"))
        semC = _stack.enter_context(nc.semaphore("semC"))
        semW0 = _stack.enter_context(nc.semaphore("semW0"))
        semW1 = _stack.enter_context(nc.semaphore("semW1"))
        semO = _stack.enter_context(nc.semaphore("semO"))
        block = _stack.enter_context(nc.Block())
        ps = [ps0, ps1, ps2, ps3, ps4, ps5, ps6, ps7]
        S = Sched({"T": semT, "V": semV, "S": semS, "X": semX, "C": semC,
                   "W0": semW0, "W1": semW1, "O": semO})

        S.op("sync", lambda e: e.dma_start(out=battn[:], in_=battn_in), inc="C", inc_by=16)
        S.op("sync", lambda e: e.dma_start(out=bproj[:], in_=bproj_in), inc="C", inc_by=16)
        S.op("sync", lambda e: e.dma_start(out=bffn[:], in_=bffn_in), inc="C", inc_by=16)
        S.op("sync", lambda e: e.dma_start(out=alpha_t[:], in_=alpha_in), inc="C", inc_by=16)
        S.op("sync", lambda e: e.dma_start(out=flag_t[:], in_=flag_in), inc="C", inc_by=16)
        S.op("sync", lambda e: e.dma_start(out=amask[:], in_=amask_in), inc="C", inc_by=16)
        S.op("sync", lambda e: e.dma_start(out=wcolt[:], in_=wcol_in), inc="C", inc_by=16)
        S.op("sync", lambda e: e.dma_start(out=ident[:], in_=ident_in), inc="C", inc_by=16)
        consts_c = S.c("C")

        S.wait("vector", "C", consts_c)
        S.op("vector", lambda e: e.memset(ones[:], 1.0))
        S.op("vector", lambda e: e.memset(epst[:], 1e-5))
        S.op("vector", lambda e: e.memset(epsb[:], 1e-5))
        if _DUMP:
            for buf in (lnbuf, x1buf, qbuf, kcache, vcache):
                S.op("vector", lambda e, buf=buf: e.memset(buf[:], 0.0))
        S.op("vector", lambda e: e.memset(onesr[:], 1.0))
        S.op("vector", lambda e: e.memset(bdq[:], 0.0))
        S.op("vector", lambda e: e.tensor_scalar_mul(halpha[:], alpha_t[:], 0.5))
        S.op("vector", lambda e: e.tensor_scalar_mul(hbffn[:], bffn[:], 0.5))
        S.op("vector", lambda e: e.tensor_scalar_mul(battn_q[:], battn[:, 0:8], 0.125))
        S.op("vector", lambda e: e.tensor_scalar_mul(bkv_fl[:], battn[:, 8:24], flag_t[:, 0:1]),
             inc="V")

        qrel = [("V", 0), ("V", 0)]
        for p in range(8):
            sl = p % 2
            S.wait("sync", *qrel[sl])
            S.op("sync", lambda e, p=p, sl=sl: e.dma_start(
                out=panel[:, 1024 * sl:1024 * sl + 1024],
                in_=wf_in[p].rearrange("p k m -> p (k m)")), inc="W%d" % sl, inc_by=16)
            S.wait("vector", "W%d" % sl, S.c("W%d" % sl))
            vv = S.op("vector", lambda e, p=p, sl=sl: e.reduce_sum(
                absacc[:, p:p + 1], panel[:, 1024 * sl:1024 * sl + 1024],
                axis=AX, apply_absolute_value=True), inc="V")
            qrel[sl] = ("V", vv)
        v_abs = S.c("V")
        S.wait("tensor", "V", v_abs)
        S.op("tensor", lambda e: e.matmul(ps0[0:1, 0:8], ones[:, 0:1], absacc[:], start=True, stop=True), inc="T")
        S.wait("vector", "T", S.c("T"))
        vv = S.op("vector", lambda e: e.reduce_sum(stats[0:1, 0:1], ps0[0:1, 0:8], axis=AX), inc="V")
        S.wait_force("vector", "V", vv)
        S.op("vector", lambda e: e.tensor_scalar_mul(stats[0:1, 4:5], stats[0:1, 0:1], 0.7 / (1024.0 * 1024.0)),
             inc="V")
        S.wait("tensor", "V", S.c("V"))
        S.op("tensor", lambda e: e.matmul(ps1[:, 0:1], onesr[:], stats[0:1, 4:5], start=True, stop=True), inc="T")
        S.wait("vector", "T", S.c("T"))
        S.op("vector", lambda e: e.tensor_copy(dquant[:, 0:1], ps1[:, 0:1]), inc="V")
        S.wait_force("vector", "V", S.c("V"))
        S.op("vector", lambda e: e.tensor_scalar_mul(dquant[:, 1:2], dquant[:, 0:1], -1.0), inc="V")
        S.wait_force("vector", "V", S.c("V"))
        wtern3 = wtern[:].rearrange("p (k m) -> p k m", k=8)
        for p in range(8):
            sl = p % 2
            S.wait("sync", *qrel[sl])
            S.op("sync", lambda e, p=p, sl=sl: e.dma_start(
                out=panel[:, 1024 * sl:1024 * sl + 1024],
                in_=wf_in[p].rearrange("p k m -> p (k m)")), inc="W%d" % sl, inc_by=16)
            S.wait("vector", "W%d" % sl, S.c("W%d" % sl))
            for h in range(2):
                S.op("vector", lambda e, sl=sl, h=h: e.tensor_single_scalar(
                    terntmp[:, 0:512], panel[:, 1024 * sl + 512 * h:1024 * sl + 512 * h + 512],
                    dquant[:, 0:1], OP.is_gt))
                S.op("vector", lambda e, sl=sl, h=h: e.tensor_single_scalar(
                    terntmp[:, 512:1024], panel[:, 1024 * sl + 512 * h:1024 * sl + 512 * h + 512],
                    dquant[:, 1:2], OP.is_lt))
                S.op("vector", lambda e, p=p, h=h: e.tensor_sub(
                    wtern3[:, 4 * h:4 * h + 4, 128 * p:128 * p + 128],
                    terntmp[:, 0:512].rearrange("p (k m) -> p k m", k=4),
                    terntmp[:, 512:1024].rearrange("p (k m) -> p k m", k=4)), inc="V")
            qrel[sl] = ("V", S.c("V"))
        v_init_done = S.c("V")

        rel = {
            "xbuf": ("V", 0), "x1buf": ("O", 0),
            "pan0": ("V", v_init_done), "pan1": ("V", v_init_done),
            "ps0": ("V", v_init_done), "ps1": ("V", v_init_done),
            "ps2": ("V", 0), "ps3": ("V", 0), "ps4": ("V", 0), "ps5": ("V", 0),
            "ps6": ("V", 0), "ps7": ("V", 0),
            "hilo0": ("T", 0), "hilo1": ("T", 0),
            "whilo0": ("T", 0), "whilo1": ("T", 0),
            "sq0": ("T", 0), "sq1": ("T", 0),
        }
        att_bd = [("T", 0), ("T", 0)]
        att_msk = [("S", 0), ("S", 0)]
        att_cb = [("T", 0), ("T", 0)]

        def emit_x_load(cidx):
            t0 = TC * (cidx + 1)
            sem, val = rel["xbuf"]
            S.wait("sync", sem, val)
            for k in range(KC):
                S.op("sync", lambda e, k=k, t0=t0: e.dma_start(
                    out=xbuf[:, 512 * k:512 * k + 512],
                    in_=x_in[128 * k:128 * k + 128, t0:t0 + 512]), inc="X", inc_by=16)
            return S.c("X")

        x_ready = emit_x_load(-1)

        def emit_layernorm(src, dst, src_ready=None, post_k=None, dual_sq=False):
            if src_ready is None:
                S.wait("tensor", "V", S.c("V"))
            S.wait("tensor", *rel["ps6"])
            for k in range(KC):
                if src_ready is not None:
                    S.wait("tensor", *src_ready[k])
                S.op("tensor", lambda e, k=k: e.matmul(
                    ps6[0:1, :], ones[:, 0:1], src[:, 512 * k:512 * k + 512],
                    start=(k == 0), stop=(k == KC - 1)))
            S.wait("tensor", *rel["ps7"])
            def sqslot(k):
                base = sq8 if (k % 4) < 2 else terntmp
                return base[:, 512 * (k % 2):512 * (k % 2) + 512]
            sq_t = {}
            t_stats = None
            t_entry = S.c("T")
            first_scalar = True
            for k in range(KC):
                eng = "scalar" if (dual_sq and (k % 2 == 1)) else "vector"
                sem = "S" if eng == "scalar" else "V"
                if k - 4 in sq_t:
                    S.wait(eng, "T", sq_t[k - 4])
                if eng == "scalar":
                    if first_scalar:
                        S.wait("scalar", "T", t_entry)
                        first_scalar = False
                    if src_ready is not None:
                        S.wait("scalar", *src_ready[k])
                    vv = S.op("scalar", lambda e, k=k: e.activation(
                        sqslot(k), src[:, 512 * k:512 * k + 512], AF.Square), inc="S")
                else:
                    vv = S.op("vector", lambda e, k=k: e.tensor_mul(
                        sqslot(k), src[:, 512 * k:512 * k + 512],
                        src[:, 512 * k:512 * k + 512]), inc="V")
                S.wait("tensor", sem, vv)
                t_stats = sq_t[k] = S.op("tensor", lambda e, k=k: e.matmul(
                    ps7[0:1, :], ones[:, 0:1], sqslot(k),
                    start=(k == 0), stop=(k == KC - 1)), inc="T")
            S.wait("scalar", "T", t_stats)
            A = sqbuf[:, 0:512]
            B = sqbuf[:, 512:1024]
            sr = {}
            for h in (0, 1):
                c0, c1 = 256 * h, 256 * h + 256
                S.op("scalar", lambda e, c0=c0, c1=c1: e.activation(
                    stats[0:1, c0:c1], ps6[0:1, c0:c1], AF.Identity))
                sr[h] = S.op("scalar", lambda e, c0=c0, c1=c1: e.activation(
                    stats[0:1, 512 + c0:512 + c1], ps7[0:1, c0:c1], AF.Identity), inc="S")
            rel["ps6"] = ("S", sr[1])
            rel["ps7"] = ("S", sr[1])
            banks = {0: ps5, 1: ps4}
            relkey = {0: "ps5", 1: "ps4"}
            tb = {}
            for h in (0, 1):
                c0, c1 = 256 * h, 256 * h + 256
                bank = banks[h]
                S.wait("tensor", "S", sr[h])
                S.wait("tensor", *rel[relkey[h]])
                S.op("tensor", lambda e, c0=c0, c1=c1, bank=bank: e.matmul(
                    bank[:, 0:256], onesr[:], stats[0:1, c0:c1], start=True, stop=False))
                tb[h] = S.op("tensor", lambda e, c0=c0, c1=c1, bank=bank: e.matmul(
                    bank[:, 256:512], onesr[:], stats[0:1, 512 + c0:512 + c1],
                    start=False, stop=True), inc="T")
            sq_f = {}
            for h in (0, 1):
                c0, c1 = 256 * h, 256 * h + 256
                S.wait("scalar", "T", tb[h])
                sq_f[h] = S.op("scalar", lambda e, c0=c0, c1=c1, bank=banks[h]: e.activation(
                    A[:, c0:c1], bank[:, 0:256], AF.Square, scale=1.0 / 1024.0), inc="S")
            vv_f = {}
            for h in (0, 1):
                c0, c1 = 256 * h, 256 * h + 256
                S.wait("vector", "S", sq_f[h])
                vv_f[h] = S.op("vector", lambda e, c0=c0, c1=c1, bank=banks[h]: e.scalar_tensor_tensor(
                    B[:, c0:c1], bank[:, 256:512], 1.0 / 1024.0, A[:, c0:c1],
                    OP.mult, OP.subtract), inc="V")
            ex_f = {}
            for h in (0, 1):
                c0, c1 = 256 * h, 256 * h + 256
                S.wait("scalar", "V", vv_f[h])
                S.op("scalar", lambda e, c0=c0, c1=c1: e.activation(
                    A[:, c0:c1], B[:, c0:c1], AF.Ln, bias=epsb[:, 0:1]))
                ex_f[h] = S.op("scalar", lambda e, c0=c0, c1=c1: e.activation(
                    B[:, c0:c1], A[:, c0:c1], AF.Exp, scale=-0.5), inc="S")
            v_mr = None
            for h in (0, 1):
                c0, c1 = 256 * h, 256 * h + 256
                S.wait("vector", "S", ex_f[h])
                v_mr = S.op("vector", lambda e, c0=c0, c1=c1, bank=banks[h]: e.scalar_tensor_tensor(
                    A[:, c0:c1], bank[:, 0:256], 1.0 / 1024.0, B[:, c0:c1],
                    OP.mult, OP.mult), inc="V")
            rel["ps4"] = ("V", v_mr)
            rel["ps5"] = ("V", v_mr)
            gates = []
            for k in range(KC):
                S.op("vector", lambda e, k=k: e.tensor_mul(dst[:, 512 * k:512 * k + 512],
                                                           src[:, 512 * k:512 * k + 512], B))
                vv = S.op("vector", lambda e, k=k: e.tensor_sub(dst[:, 512 * k:512 * k + 512],
                                                                dst[:, 512 * k:512 * k + 512], A),
                     inc="V")
                gates.append(("V", vv))
                if post_k is not None and k < 2:
                    post_k(k)
            return gates

        def sqslot2(k):
            base = sq8 if (k % 4) < 2 else terntmp
            return base[:, 512 * (k % 2):512 * (k % 2) + 512]

        def emit_ln_stats2(src, src_ready):
            S.wait("tensor", *rel["ps6"])
            for k in range(KC):
                if src_ready is not None:
                    S.wait("tensor", *src_ready[k])
                S.op("tensor", lambda e, k=k: e.matmul(
                    ps6[0:1, :], ones[:, 0:1], src[:, 512 * k:512 * k + 512],
                    start=(k == 0), stop=(k == KC - 1)))
            S.wait("tensor", *rel["ps7"])
            sq_t = {}
            t_stats = None
            for k in range(KC):
                if k - 4 in sq_t:
                    S.wait("vector", "T", sq_t[k - 4])
                vv = S.op("vector", lambda e, k=k: e.tensor_mul(
                    sqslot2(k), src[:, 512 * k:512 * k + 512],
                    src[:, 512 * k:512 * k + 512]), inc="V")
                S.wait("tensor", "V", vv)
                t_stats = sq_t[k] = S.op("tensor", lambda e, k=k: e.matmul(
                    ps7[0:1, :], ones[:, 0:1], sqslot2(k),
                    start=(k == 0), stop=(k == KC - 1)), inc="T")
            S.wait("vector", "T", t_stats)
            S.op("vector", lambda e: e.tensor_scalar_mul(stats[0:1, 0:512], ps6[0:1, :], 1.0 / 1024.0))
            S.op("vector", lambda e: e.tensor_mul(stats[0:1, 512:1024], stats[0:1, 0:512], stats[0:1, 0:512]))
            v_m = S.op("vector", lambda e: e.scalar_tensor_tensor(
                stats[0:1, 1024:1536], ps7[0:1, :], 1.0 / 1024.0, stats[0:1, 512:1024],
                OP.mult, OP.subtract), inc="V")
            rel["ps6"] = ("V", v_m)
            rel["ps7"] = ("V", v_m)
            S.wait("scalar", "V", v_m)
            S.op("scalar", lambda e: e.activation(stats[0:1, 512:1024], stats[0:1, 1024:1536], AF.Ln, bias=epst[0:1, 0:1]))
            s_r = S.op("scalar", lambda e: e.activation(cts[0:1, 0:512], stats[0:1, 512:1024], AF.Exp, scale=-0.5),
                       inc="S")
            S.wait("vector", "S", s_r)
            v_mr = S.op("vector", lambda e: e.tensor_mul(
                cts[0:1, 512:1024], stats[0:1, 0:512], cts[0:1, 0:512]), inc="V")
            return v_mr

        def emit_ln_center2(src, dst, v_mr, post_k=None):
            S.wait("tensor", "V", v_mr)
            S.wait("tensor", *rel["ps6"])
            S.wait("tensor", *rel["ps7"])
            S.op("tensor", lambda e: e.matmul(ps6[:, :], onesr[:], cts[0:1, 0:512], start=True, stop=True))
            t_bc = S.op("tensor", lambda e: e.matmul(ps7[:, :], onesr[:], cts[0:1, 512:1024], start=True, stop=True),
                        inc="T")
            S.wait("vector", "T", t_bc)
            gates = []
            for k in range(KC):
                sq = sqbuf[:, 512 * (k % 2):512 * (k % 2) + 512]
                S.op("vector", lambda e, k=k, sq=sq: e.tensor_mul(sq, src[:, 512 * k:512 * k + 512], ps6[:, :]))
                vv = S.op("vector", lambda e, k=k, sq=sq: e.tensor_sub(dst[:, 512 * k:512 * k + 512], sq, ps7[:, :]),
                     inc="V")
                gates.append(("V", vv))
                if post_k is not None and k < 2:
                    post_k(k)
            v_ln = S.c("V")
            rel["ps6"] = ("V", v_ln)
            rel["ps7"] = ("V", v_ln)
            return gates

        def make_presplit(src, store):
            def post_k(k):
                pr = k % 2
                hi = hibuf[:, 512 * pr:512 * pr + 512]
                lo = lobuf[:, 512 * pr:512 * pr + 512]
                S.wait("vector", *rel["hilo%d" % pr])
                S.op("vector", lambda e, k=k, hi=hi: e.tensor_copy(hi, src[:, 512 * k:512 * k + 512]))
                store[k] = S.op("vector", lambda e, k=k, hi=hi, lo=lo: e.tensor_sub(
                    lo, src[:, 512 * k:512 * k + 512], hi), inc="V")
            return post_k

        def emit_ffn(src, resid, dst, pre_split=None):
            for i in range(8):
                S.wait("tensor", *rel["ps%d" % i])
            bank_t = {}
            for k in range(KC):
                pr = k % 2
                hi = hibuf[:, 512 * pr:512 * pr + 512]
                lo = lobuf[:, 512 * pr:512 * pr + 512]
                if pre_split is not None and k in pre_split:
                    S.wait("tensor", "V", pre_split[k])
                else:
                    S.wait("vector", *rel["hilo%d" % pr])
                    S.op("vector", lambda e, k=k, hi=hi: e.tensor_copy(hi, src[:, 512 * k:512 * k + 512]))
                    S.op("vector", lambda e, k=k, hi=hi, lo=lo: e.tensor_sub(lo, src[:, 512 * k:512 * k + 512], hi),
                         inc="V")
                    S.wait("tensor", "V", S.c("V"))
                last = (k == KC - 1)
                first = (k == 0)
                tval = None
                for m in range(8):
                    wt = wtern[:, 1024 * k + 128 * m:1024 * k + 128 * m + 128]
                    S.op("tensor", lambda e, m=m, wt=wt, hi=hi, first=first: e.matmul(
                        ps[m][:, :], wt, hi, start=first, stop=False))
                    tval = S.op("tensor", lambda e, m=m, wt=wt, lo=lo, last=last: e.matmul(
                        ps[m][:, :], wt, lo, start=False, stop=last),
                        inc="T" if (m == 7 or last) else None)
                    if last:
                        bank_t[m] = tval
                rel["hilo%d" % pr] = ("T", tval)
            t_ffn = S.c("T")
            S.wait("vector", "T", t_ffn)
            sem, val = rel["x1buf"] if dst is x1buf else ("V", 0)
            if val:
                S.wait("vector", sem, val)
            evac_gates = []
            evac_vv = {}
            for m in range(8):
                sq = sqbuf[:, 512 * (m % 2):512 * (m % 2) + 512]
                S.wait("scalar", "T", bank_t[m])
                if m - 2 in evac_vv:
                    S.wait("scalar", "V", evac_vv[m - 2])
                sv = S.op("scalar", lambda e, m=m, sq=sq: e.activation(
                    sq, ps[m][:, :], AF.Identity, bias=hbffn[:, m:m + 1], scale=halpha[:, 0:1]), inc="S")
                S.wait("vector", "S", sv)
                vv = S.op("vector", lambda e, m=m, sq=sq: e.tensor_add(
                    dst[:, 512 * m:512 * m + 512], sq, resid[:, 512 * m:512 * m + 512]), inc="V")
                evac_vv[m] = vv
                evac_gates.append(("V", vv))
            v_ffn = S.c("V")
            for i in range(8):
                rel["ps%d" % i] = ("V", v_ffn)
            rel["sq0"] = ("V", v_ffn)
            rel["sq1"] = ("V", v_ffn)
            return evac_gates

        def emit_wmm(w_in, groups, src, evac_fn, v_src, src_gates=None, pre_split=None,
                     filler_factory=None, gates_out=None):
            gates = gates_out if gates_out is not None else {}
            slab_done = {}

            def issue_slab(g, k, si):
                sl = si % 2
                S.wait("sync", *rel["pan%d" % sl])
                S.op("sync", lambda e, g=g, k=k, sl=sl: e.dma_start(
                    out=panel[:, 1024 * sl:1024 * sl + 1024],
                    in_=w_in[g, k].rearrange("p k m -> p (k m)")), inc="W%d" % sl, inc_by=16)
                slab_done[(g, k)] = S.c("W%d" % sl)

            seq = [(g, k) for g in groups for k in range(KC)]
            issue_slab(*seq[0], 0)
            if len(seq) > 1:
                issue_slab(*seq[1], 1)
            filler = None
            for si, (g, k) in enumerate(seq):
                sl = si % 2
                if filler_factory is not None and g == groups[-1] and k == 0:
                    filler = filler_factory()
                    filler_factory = None
                S.wait("vector", "W%d" % sl, slab_done[(g, k)])
                S.wait("vector", *rel["whilo%d" % sl])
                S.op("vector", lambda e, sl=sl: e.tensor_copy(
                    whilo[:, 2048 * sl:2048 * sl + 1024], panel[:, 1024 * sl:1024 * sl + 1024]))
                v_wsplit = S.op("vector", lambda e, sl=sl: e.tensor_sub(
                    whilo[:, 2048 * sl + 1024:2048 * sl + 2048], panel[:, 1024 * sl:1024 * sl + 1024],
                    whilo[:, 2048 * sl:2048 * sl + 1024]), inc="V")
                pr = k % 2
                hi = hibuf[:, 512 * pr:512 * pr + 512]
                lo = lobuf[:, 512 * pr:512 * pr + 512]
                if pre_split is not None and si < 2 and k in pre_split:
                    if k == 0:
                        for i in range(8):
                            S.wait("tensor", *rel["ps%d" % i])
                    S.wait("tensor", "V", max(pre_split[k], v_wsplit))
                else:
                    S.wait("vector", *rel["hilo%d" % pr])
                    if src_gates is not None:
                        S.wait("vector", *src_gates[k])
                    else:
                        S.wait("vector", "V", v_src)
                    S.op("vector", lambda e, k=k, hi=hi: e.tensor_copy(hi, src[:, 512 * k:512 * k + 512]))
                    S.op("vector", lambda e, k=k, hi=hi, lo=lo: e.tensor_sub(
                        lo, src[:, 512 * k:512 * k + 512], hi), inc="V")
                    if k == 0:
                        for i in range(8):
                            S.wait("tensor", *rel["ps%d" % i])
                    S.wait("tensor", "V", S.c("V"))
                tval = None
                last = (k == KC - 1)
                bank_t = {}
                for p in range(8):
                    whi = whilo[:, 2048 * sl + 128 * p:2048 * sl + 128 * p + 128]
                    wlo = whilo[:, 2048 * sl + 1024 + 128 * p:2048 * sl + 1024 + 128 * p + 128]
                    S.op("tensor", lambda e, p=p, whi=whi, hi=hi, k=k: e.matmul(
                        ps[p][:, :], whi, hi, start=(k == 0), stop=False))
                    S.op("tensor", lambda e, p=p, wlo=wlo, hi=hi: e.matmul(
                        ps[p][:, :], wlo, hi, start=False, stop=False))
                    tval = S.op("tensor", lambda e, p=p, whi=whi, lo=lo, last=last: e.matmul(
                        ps[p][:, :], whi, lo, start=False, stop=last),
                        inc="T" if (p == 7 or last) else None)
                    if last:
                        bank_t[p] = tval
                rel["hilo%d" % pr] = ("T", tval)
                rel["whilo%d" % sl] = ("T", tval)
                rel["pan%d" % sl] = ("V", v_wsplit)
                if filler:
                    for _ in range(min(18, len(filler))):
                        filler.pop(0)()
                if si + 2 < len(seq):
                    issue_slab(*seq[si + 2], si + 2)
                if k == KC - 1:
                    for p in range(8):
                        gates[8 * g + p] = evac_fn(g, p, ps[p], bank_t[p])
            return gates, filler

        def emit_qkv(warm, ln2_gates=None, qkv_pre=None, filler_factory=None, gates_out=None):
            groups = (1, 2) if warm else (1, 2, 0)
            v_ln2 = S.c("V")

            def evac(g, p, bank, tval):
                pid = 8 * g + p
                S.wait("scalar", "T", tval)
                if pid < 8:
                    sval = S.op("scalar", lambda e, pid=pid, bank=bank: e.activation(
                        qbuf[:, 512 * pid:512 * pid + 512], bank[:, :], AF.Identity,
                        bias=battn_q[:, pid:pid + 1], scale=0.125), inc="S")
                else:
                    cache = kcache if pid < 16 else vcache
                    j = (pid - 8) % 8
                    if warm:
                        sval = S.op("scalar", lambda e, bank=bank, cache=cache, j=j, pid=pid: e.activation(
                            cache[:, CACHE_W * j + 64:CACHE_W * j + 576], bank[:, :], AF.Identity,
                            bias=bkv_fl[:, pid - 8:pid - 7], scale=flag_t[:, 0:1]), inc="S")
                    else:
                        sval = S.op("scalar", lambda e, bank=bank, cache=cache, j=j, pid=pid: e.activation(
                            cache[:, CACHE_W * j + 64:CACHE_W * j + 576], bank[:, :], AF.Identity,
                            bias=battn[:, pid:pid + 1]), inc="S")
                rel["ps%d" % p] = ("S", sval)
                return ("S", sval)

            return emit_wmm(wq_in, groups, lnbuf, evac, v_ln2, src_gates=ln2_gates, pre_split=qkv_pre,
                            filler_factory=filler_factory, gates_out=gates_out)

        def make_pooling_thunks(kv_gates=None):
            thunks = []

            def T(fn):
                thunks.append(fn)

            s_qkv = S.c("S")
            combos = [(p, kv) for kv in range(2) for p in range(NPAIR)]
            cach = (kcache, vcache)
            rngs = (krings, vrings)

            def h1ap(i):
                j = i % 8
                base = sq8 if j < 4 else terntmp
                return base[:, 256 * (j % 4):256 * (j % 4) + 256]

            def h2ap(i):
                return cts[:, 128 * (i % 8):128 * (i % 8) + 128]

            for half in (0, 1):
                for i in range(8 * half, 8 * half + 8):
                    p, kv = combos[i]
                    cur = cach[kv][:, CACHE_W * p + 64:CACHE_W * p + 576]

                    def t_h1(i=i, p=p, kv=kv, cur=cur):
                        if kv_gates is not None:
                            S.wait("vector", *kv_gates[(8 if kv == 0 else 16) + p])
                        else:
                            S.wait("vector", "S", s_qkv)
                        S.op("vector", lambda e, cur=cur, i=i: e.tensor_add(
                            h1ap(i), cur[:, 0:512:2], cur[:, 1:512:2]))
                    T(t_h1)
                for i in range(8 * half, 8 * half + 8):
                    T(lambda i=i: S.op("vector", lambda e, i=i: e.tensor_add(
                        h2ap(i), h1ap(i)[:, 0:256:2], h1ap(i)[:, 1:256:2])))
                for i in range(8 * half, 8 * half + 8):
                    p, kv = combos[i]
                    r0 = RING_W * p
                    T(lambda i=i, ring=rngs[kv], r0=r0: S.op("vector", lambda e, i=i, ring=ring, r0=r0: e.tensor_add(
                        ring[:, r0 + 8:r0 + 72], h2ap(i)[:, 0:128:2], h2ap(i)[:, 1:128:2])))
            for i, (p, kv) in enumerate(combos):
                r0 = RING_W * p
                T(lambda ring=rngs[kv], r0=r0: S.op("vector", lambda e, ring=ring, r0=r0: e.tensor_add(
                    ring[:, r0 + 80:r0 + 112], ring[:, r0 + 8:r0 + 72:2], ring[:, r0 + 9:r0 + 72:2])))
            for i, (p, kv) in enumerate(combos):
                r0 = RING_W * p
                T(lambda ring=rngs[kv], r0=r0: S.op("vector", lambda e, ring=ring, r0=r0: e.tensor_add(
                    ring[:, r0 + 120:r0 + 136], ring[:, r0 + 80:r0 + 112:2], ring[:, r0 + 81:r0 + 112:2])))
            for i, (p, kv) in enumerate(combos):
                r0 = RING_W * p
                T(lambda ring=rngs[kv], r0=r0: S.op("vector", lambda e, ring=ring, r0=r0: e.tensor_add(
                    ring[:, r0 + 144:r0 + 152], ring[:, r0 + 120:r0 + 136:2], ring[:, r0 + 121:r0 + 136:2])))
            nsc = [0]
            for sc, lo_, hi_ in ((0.125, 8, 72), (1.0 / 16.0, 80, 112),
                                 (1.0 / 32.0, 120, 136), (1.0 / 64.0, 144, 152)):
                for p in range(NPAIR):
                    r0 = RING_W * p
                    nsc[0] += 1
                    last = (nsc[0] == 32)

                    def t_sc2(r0=r0, sc=sc, lo_=lo_, hi_=hi_, last=last):
                        S.op("vector", lambda e, r0=r0, sc=sc, lo_=lo_, hi_=hi_: e.tensor_scalar_mul(
                            krings[:, r0 + lo_:r0 + hi_], krings[:, r0 + lo_:r0 + hi_], sc),
                            inc="V" if last else None)
                    T(t_sc2)
            return thunks

        def emit_pooling(kv_gates=None, thunks=None):
            if thunks is None:
                thunks = make_pooling_thunks(kv_gates)
            while thunks:
                thunks.pop(0)()
            return S.c("V")

        def emit_ring_shift(t_gate=None):
            S.wait("vector", "T", t_gate if t_gate is not None else S.c("T"))
            for p in range(NPAIR):
                for ring in (krings, vrings):
                    r0 = RING_W * p
                    S.op("vector", lambda e, ring=ring, r0=r0: e.tensor_copy(ring[:, r0 + 0:r0 + 8], ring[:, r0 + 64:r0 + 72]))
                    S.op("vector", lambda e, ring=ring, r0=r0: e.tensor_copy(ring[:, r0 + 72:r0 + 80], ring[:, r0 + 104:r0 + 112]))
                    S.op("vector", lambda e, ring=ring, r0=r0: e.tensor_copy(ring[:, r0 + 112:r0 + 120], ring[:, r0 + 128:r0 + 136]))
                    S.op("vector", lambda e, ring=ring, r0=r0: e.tensor_copy(ring[:, r0 + 136:r0 + 144], ring[:, r0 + 144:r0 + 152]))
                for cache in (kcache, vcache):
                    c0 = CACHE_W * p
                    S.op("vector", lambda e, cache=cache, c0=c0: e.tensor_copy(
                        cache[:, c0:c0 + 64], cache[:, c0 + 512:c0 + 576]))
            return S.op("vector", lambda e: e.memset(small[:, 6:7], 0.0), inc="V")

        def emit_attention(kv_gates=None):
            s_qkv = S.c("S")
            NU = NPAIR * NGRP
            t_sc, v_msk, s_exp, v_es, s_c, t_tr, s_vt, t_val, s_out = ({} for _ in range(9))

            def st_stage(u):
                p, g = divmod(u, NGRP)
                sl = u % 3
                q0 = 512 * p + 64 * g
                sl8 = u % 6
                if kv_gates is not None:
                    S.wait("vector", *kv_gates[p])
                else:
                    S.wait("vector", "S", s_qkv)
                if u - 3 in t_sc:
                    S.wait("vector", "T", t_sc[u - 3])
                if u - 6 in t_tr:
                    S.wait("vector", "T", t_tr[u - 6])
                S.op("vector", lambda e, q0=q0, sl=sl: e.tensor_copy(
                    bdq[0:64, 128 * sl:128 * sl + 64], qbuf[0:64, q0:q0 + 64]))
                S.op("vector", lambda e, q0=q0, sl=sl: e.tensor_copy(
                    bdq[64:128, 128 * sl + 64:128 * sl + 128], qbuf[64:128, q0:q0 + 64]))
                for j, lvl in enumerate((1, 2, 3, 4)):
                    a, b = _ring_slice(g, lvl)
                    S.op("vector", lambda e, p=p, a=a, b=b, j=j, sl8=sl8: e.tensor_copy(
                        vcomb[:, 32 * sl8 + 8 * j:32 * sl8 + 8 * j + 8],
                        vrings[:, RING_W * p + a:RING_W * p + b]))
                vs = None
                for j, lvl in enumerate((1, 2, 3, 4)):
                    a, b = _ring_slice(g, lvl)
                    vs = S.op("vector", lambda e, p=p, a=a, b=b, j=j, sl=sl: e.tensor_copy(
                        kcomb[:, 32 * sl + 8 * j:32 * sl + 8 * j + 8],
                        krings[:, RING_W * p + a:RING_W * p + b]), inc="V" if j == 3 else None)
                return vs

            def psS_ap(u):
                return ps[u % 2][:, 256 * ((u // 2) % 2):256 * ((u // 2) % 2) + 160]

            def st_scores(u, v_stage):
                p, g = divmod(u, NGRP)
                sl = u % 3
                kw0 = CACHE_W * p + 64 * g
                if kv_gates is not None:
                    S.wait("tensor", *kv_gates[8 + p])
                S.wait("tensor", "V", v_stage)
                if u - 4 in v_msk:
                    S.wait("tensor", "V", v_msk[u - 4])
                pS = psS_ap(u)
                S.op("tensor", lambda e, sl=sl, pS=pS, kw0=kw0: e.matmul(
                    pS[:, 0:128], bdq[:, 128 * sl:128 * sl + 128], kcache[:, kw0:kw0 + 128],
                    start=True, stop=False))
                t_sc[u] = S.op("tensor", lambda e, sl=sl, pS=pS: e.matmul(
                    pS[:, 128:160], bdq[:, 128 * sl:128 * sl + 128],
                    kcomb[:, 32 * sl:32 * sl + 32],
                    start=False, stop=True), inc="T")

            def st_mask(u):
                pi = u % 2
                S.wait("vector", "T", t_sc[u])
                if u - 2 in s_exp:
                    S.wait("vector", "S", s_exp[u - 2])
                pS = psS_ap(u)
                S.op("vector", lambda e, pi=pi, pS=pS: e.tensor_add(
                    mskS[:, 160 * pi:160 * pi + 160], pS[:, 0:160], amask[:]))
                v_msk[u] = S.op("vector", lambda e, pi=pi: e.reduce_max(
                    small[:, pi:pi + 1], mskS[:, 160 * pi:160 * pi + 160], axis=AX, negate=True), inc="V")

            def st_exp(u):
                pi = u % 2
                S.wait("scalar", "V", v_msk[u])
                s_exp[u] = S.op("scalar", lambda e, pi=pi: e.activation(
                    ebuf[:, 160 * pi:160 * pi + 160], mskS[:, 160 * pi:160 * pi + 160], AF.Exp,
                    bias=small[:, pi:pi + 1]), inc="S")

            def st_eprc(u):
                pi = u % 2
                S.wait("vector", "S", s_exp[u])
                if u - 2 in s_c:
                    S.wait("vector", "S", s_c[u - 2])
                S.op("vector", lambda e, pi=pi: e.tensor_mul(
                    eprb[:, 160 * pi:160 * pi + 160], ebuf[:, 160 * pi:160 * pi + 160], wcolt[:]))
                vr = S.op("vector", lambda e, pi=pi: e.reduce_sum(
                    small[:, 2 + pi:3 + pi], eprb[:, 160 * pi:160 * pi + 160], axis=AX), inc="V")
                S.wait_force("vector", "V", vr)
                v_es[u] = S.op("vector", lambda e, pi=pi: e.reciprocal(
                    small[:, 4 + pi:5 + pi], small[:, 2 + pi:3 + pi]), inc="V")

            def st_C(u):
                pi = u % 2
                S.wait("scalar", "V", v_es[u])
                if u - 2 in t_tr:
                    S.wait("scalar", "T", t_tr[u - 2])
                s_c[u] = S.op("scalar", lambda e, pi=pi: e.activation(
                    cbuf[:, 160 * pi:160 * pi + 160], ebuf[:, 160 * pi:160 * pi + 160], AF.Identity,
                    scale=small[:, 4 + pi:5 + pi]), inc="S")

            def st_tr(u):
                p, g = divmod(u, NGRP)
                pi, sl = u % 2, u % 6
                kw0 = CACHE_W * p + 64 * g
                psT = ps[2 + pi]
                if kv_gates is not None:
                    S.wait("tensor", *kv_gates[16 + p])
                S.wait("tensor", "S", s_c[u])
                if u - 2 in s_vt:
                    S.wait("tensor", "S", s_vt[u - 2])
                S.op("tensor", lambda e, pi=pi, psT=psT: e.transpose(
                    psT[:, 0:128], cbuf[:, 160 * pi:160 * pi + 128], ident[:]))
                S.op("tensor", lambda e, pi=pi, psT=psT: e.transpose(
                    psT[0:32, 128:256], cbuf[:, 160 * pi + 128:160 * pi + 160], ident[:]))
                S.op("tensor", lambda e, psT=psT, kw0=kw0: e.transpose(
                    psT[:, 256:384], vcache[:, kw0:kw0 + 128], ident[:]))
                t_tr[u] = S.op("tensor", lambda e, psT=psT, sl=sl: e.transpose(
                    psT[0:32, 384:512], vcomb[:, 32 * sl:32 * sl + 32], ident[:]), inc="T")

            def st_evac(u):
                pi = u % 2
                psT = ps[2 + pi]
                S.wait("scalar", "T", t_tr[u])
                if u - 2 in t_val:
                    S.wait("scalar", "T", t_val[u - 2])
                S.op("scalar", lambda e, pi=pi, psT=psT: e.activation(
                    cts[:, 512 * pi:512 * pi + 128], psT[:, 0:128], AF.Identity))
                S.op("scalar", lambda e, pi=pi, psT=psT: e.activation(
                    cts[0:32, 512 * pi + 128:512 * pi + 256], psT[0:32, 128:256], AF.Identity))
                S.op("scalar", lambda e, pi=pi, psT=psT: e.activation(
                    cts[:, 512 * pi + 256:512 * pi + 384], psT[:, 256:384], AF.Identity))
                s_vt[u] = S.op("scalar", lambda e, pi=pi, psT=psT: e.activation(
                    cts[0:32, 512 * pi + 384:512 * pi + 512], psT[0:32, 384:512], AF.Identity), inc="S")

            def st_value(u):
                pi = u % 2
                psO = ps[4 + pi]
                S.wait("tensor", "S", s_vt[u])
                if u - 2 in s_out:
                    S.wait("tensor", "S", s_out[u - 2])
                S.op("tensor", lambda e, pi=pi, psO=psO: e.matmul(
                    psO[:, 0:128], cts[:, 512 * pi + 256:512 * pi + 384], cts[:, 512 * pi:512 * pi + 128],
                    start=True, stop=False))
                t_val[u] = S.op("tensor", lambda e, pi=pi, psO=psO: e.matmul(
                    psO[:, 0:128], cts[0:32, 512 * pi + 384:512 * pi + 512],
                    cts[0:32, 512 * pi + 128:512 * pi + 256],
                    start=False, stop=True), inc="T")

            def st_out(u):
                p, g = divmod(u, NGRP)
                pi = u % 2
                psO = ps[4 + pi]
                S.wait("scalar", "T", t_val[u])
                S.op("scalar", lambda e, p=p, g=g, psO=psO: e.activation(
                    lnbuf[0:64, 512 * p + 64 * g:512 * p + 64 * g + 64], psO[0:64, 0:64], AF.Identity))
                s_out[u] = S.op("scalar", lambda e, p=p, g=g, psO=psO: e.activation(
                    lnbuf[64:128, 512 * p + 64 * g:512 * p + 64 * g + 64], psO[64:128, 64:128], AF.Identity), inc="S")

            v_stage = {}
            for i in range(8):
                S.wait("tensor", *rel["ps%d" % i])
            v_stage[0] = st_stage(0)
            for n in range(NU + 8):
                if n - 1 >= 0 and n - 1 < NU:
                    st_mask(n - 1)
                if n + 1 < NU:
                    v_stage[n + 1] = st_stage(n + 1)
                if n - 3 >= 0 and n - 3 < NU:
                    st_C(n - 3)
                if n - 1 >= 0 and n - 1 < NU:
                    st_exp(n - 1)
                if n - 2 >= 0 and n - 2 < NU:
                    st_eprc(n - 2)
                if n - 5 >= 0 and n - 5 < NU:
                    st_evac(n - 5)
                if n - 7 >= 0 and n - 7 < NU:
                    st_out(n - 7)
                if n - 4 >= 0 and n - 4 < NU:
                    st_tr(n - 4)
                if n - 6 >= 0 and n - 6 < NU:
                    st_value(n - 6)
                if n < NU:
                    st_scores(n, v_stage[n])
            for i, sem_map in ((0, v_msk), (1, v_msk)):
                rel["ps%d" % i] = ("V", v_msk[NU - 2 + i])
            rel["ps2"] = ("S", s_vt[NU - 2])
            rel["ps3"] = ("S", s_vt[NU - 1])
            rel["ps4"] = ("S", s_out[NU - 2])
            rel["ps5"] = ("S", s_out[NU - 1])
            return t_sc[NU - 1], s_out

        def emit_proj(attn_gates=None):
            s_attn = S.c("S")
            proj_gates = [None] * 8

            def evac(g, p, bank, tval):
                S.wait("vector", "T", tval)
                vval = S.op("vector", lambda e, p=p, bank=bank: e.scalar_tensor_tensor(
                    qbuf[:, 512 * p:512 * p + 512], bank[:, :], bproj[:, p:p + 1],
                    x1buf[:, 512 * p:512 * p + 512], OP.add, OP.add), inc="V")
                proj_gates[p] = ("V", vval)
                rel["ps%d" % p] = ("V", vval)
                return ("V", vval)

            emit_wmm(wp_in, (0,), lnbuf, evac, S.c("V"), src_gates=attn_gates)
            return proj_gates

        t_attn_last = None
        ln1_park = None
        for c in range(-1, NCHUNK):
            warm = (c == -1)
            if _STAGE < 1:
                break
            if not warm:
                emit_ring_shift(t_attn_last)
            S.wait("tensor", "X", x_ready)
            S.wait("vector", "X", x_ready)
            if warm:
                S.wait("vector", "V", v_init_done)
            ffn1_pre = {}
            if ln1_park is not None:
                emit_ln_center2(xbuf, lnbuf, ln1_park, post_k=make_presplit(lnbuf, ffn1_pre))
                ln1_park = None
            else:
                emit_layernorm(xbuf, lnbuf, post_k=make_presplit(lnbuf, ffn1_pre))
            if _STAGE < 2:
                break
            ffn1_gates = emit_ffn(lnbuf, xbuf, x1buf, pre_split=ffn1_pre)
            v_x_dead = S.c("V")
            rel["xbuf"] = ("V", v_x_dead)
            if c < NCHUNK - 1:
                x_ready = emit_x_load(c + 1)
            if _STAGE < 3:
                break
            qkv_pre = {}
            ln2_gates = emit_layernorm(x1buf, lnbuf, src_ready=ffn1_gates,
                                       post_k=make_presplit(lnbuf, qkv_pre), dual_sq=True)
            if _STAGE < 4:
                break
            kv_gates_box = {}
            ff = None if warm else (lambda: make_pooling_thunks(kv_gates_box))
            kv_gates, leftover = emit_qkv(warm, ln2_gates, qkv_pre,
                                          filler_factory=ff, gates_out=kv_gates_box)
            if _STAGE < 5:
                break
            emit_pooling(kv_gates=kv_gates, thunks=leftover)
            if warm:
                continue
            if _STAGE < 6:
                break
            t_attn_last, attn_souts = emit_attention(kv_gates)
            if _STAGE < 7:
                break
            if c < NCHUNK - 1:
                S.wait("tensor", "X", x_ready)
                S.wait("vector", "X", x_ready)
                ln1_park = emit_ln_stats2(xbuf, None)
            attn_gates = [("S", attn_souts[8 * k + 7]) for k in range(8)]
            proj_gates = emit_proj(attn_gates)
            if _STAGE < 8:
                break
            ffn2_pre = {}
            emit_layernorm(qbuf, lnbuf, src_ready=proj_gates,
                           post_k=make_presplit(lnbuf, ffn2_pre), dual_sq=True)
            S.wait("vector", "O", rel["x1buf"][1] if rel["x1buf"][0] == "O" else 0)
            emit_ffn(lnbuf, qbuf, x1buf, pre_split=ffn2_pre)
            v_out = S.c("V")
            S.wait("gpsimd", "V", v_out)
            for m in range(8):
                S.op("gpsimd", lambda e, m=m, c=c: e.dma_start(
                    out=y_out[128 * m:128 * m + 128, 512 * c:512 * c + 512],
                    in_=x1buf[:, 512 * m:512 * m + 512]), inc="O", inc_by=16)
            rel["x1buf"] = ("O", S.c("O"))

        if _DUMP:
            S.wait("gpsimd", "T", S.c("T"))
            S.wait("gpsimd", "V", S.c("V"))
            S.wait("gpsimd", "S", S.c("S"))
            for i, buf in enumerate((xbuf, lnbuf, x1buf, qbuf)):
                S.op("gpsimd", lambda e, i=i, buf=buf: e.dma_start(
                    out=dbg_out[:, 4096 * i:4096 * i + 4096], in_=buf[:]), inc="O", inc_by=16)
            S.op("gpsimd", lambda e: e.dma_start(
                out=dbg_out[:, 16384:16384 + 4608], in_=kcache[:]), inc="O", inc_by=16)
            S.op("gpsimd", lambda e: e.dma_start(
                out=dbg_out[:, 20992:20992 + 4608], in_=vcache[:]), inc="O", inc_by=16)
            S.op("gpsimd", lambda e: e.dma_start(
                out=dbg_out[:, 25600:25600 + 1024], in_=wtern[:, 0:1024].bitcast(F32)), inc="O", inc_by=16)
            S.op("gpsimd", lambda e: e.dma_start(
                out=dbg_out[:, 26624 - 16:26624 - 14], in_=dquant[:]), inc="O", inc_by=16)
            S.op("gpsimd", lambda e: e.dma_start(
                out=dbg_out[:, 26624 - 14:26624 - 6], in_=absacc[:]), inc="O", inc_by=16)
        S.wait("gpsimd", "O", S.c("O"))

        @block.tensor
        def _(e):
            for f in S.streams["tensor"]:
                f(e)

        @block.vector
        def _(e):
            for f in S.streams["vector"]:
                f(e)

        @block.scalar
        def _(e):
            for f in S.streams["scalar"]:
                f(e)

        @block.sync
        def _(e):
            for f in S.streams["sync"]:
                f(e)

        @block.gpsimd
        def _(e):
            for f in S.streams["gpsimd"]:
                f(e)

    return nc


def _host_inputs(L0_attn_x, L0_attn_W_attn, L0_attn_b_attn, L0_attn_W_proj,
                 L0_attn_b_proj, L0_ffn_weight, L0_ffn_alpha, L0_ffn_bias):
    x = np.ascontiguousarray(np.asarray(L0_attn_x, dtype=np.float32).reshape(2 * 16384, D).T)
    wq = np.ascontiguousarray(
        np.asarray(L0_attn_W_attn, np.float32).reshape(8, 128, 3, 8, 128).transpose(2, 0, 1, 3, 4))
    wp = np.ascontiguousarray(
        np.asarray(L0_attn_W_proj, np.float32).reshape(8, 128, 1, 8, 128).transpose(2, 0, 1, 3, 4))
    wfT = np.ascontiguousarray(np.asarray(L0_ffn_weight, np.float32).T)
    wf = np.ascontiguousarray(wfT.reshape(8, 128, 8, 128).transpose(2, 1, 0, 3))
    battn = np.ascontiguousarray(np.asarray(L0_attn_b_attn, np.float32).reshape(24, 128).T)
    bproj = np.ascontiguousarray(np.asarray(L0_attn_b_proj, np.float32).reshape(8, 128).T)
    bffn = np.ascontiguousarray(np.asarray(L0_ffn_bias, np.float32).reshape(8, 128).T)
    alpha = np.ascontiguousarray(np.broadcast_to(np.asarray(L0_ffn_alpha, np.float32).reshape(1, 1), (128, 1)))

    amask = np.zeros((128, 160), np.float32)
    for r in range(128):
        i = r % 64
        for j in range(128):
            if not (i + 1 <= j <= i + 64):
                amask[r, j] = NEG
    wcol = np.ones((128, 160), np.float32)
    wcol[:, 128:136] = 8.0
    wcol[:, 136:144] = 16.0
    wcol[:, 144:152] = 32.0
    wcol[:, 152:160] = 64.0
    ident = np.eye(128, dtype=np.float32)

    in_maps = []
    for core in range(NCORES):
        t0 = TOK_CORE * core
        xc = np.zeros((D, TOK_CORE + TC), np.float32)
        xc[:, TC:] = x[:, t0:t0 + TOK_CORE]
        if core % 4 != 0:
            xc[:, 0:TC] = x[:, t0 - TC:t0]
            fl = 1.0
        else:
            fl = 0.0
        flag = np.full((128, 1), fl, np.float32)
        in_maps.append({
            "x": np.ascontiguousarray(xc), "wq": wq, "wp": wp, "wf": wf,
            "battn": battn, "bproj": bproj, "bffn": bffn, "alpha": alpha,
            "flag": flag, "amask": amask, "wcol": wcol, "ident": ident,
        })
    return in_maps


_NC_CACHE = None


def kernel(L0_attn_x, L0_attn_W_attn, L0_attn_b_attn, L0_attn_W_proj,
           L0_attn_b_proj, L0_ffn_weight, L0_ffn_alpha, L0_ffn_bias):
    global _NC_CACHE
    in_maps = _host_inputs(L0_attn_x, L0_attn_W_attn, L0_attn_b_attn, L0_attn_W_proj,
                           L0_attn_b_proj, L0_ffn_weight, L0_ffn_alpha, L0_ffn_bias)
    if _NC_CACHE is None:
        _NC_CACHE = build_nc()
    res = run_bass_kernel_spmd(_NC_CACHE, in_maps, core_ids=list(range(NCORES)))
    ys = [res.results[i]["y"] for i in range(NCORES)]
    full = np.concatenate(ys, axis=1)
    return np.ascontiguousarray(full.T).reshape(2, 16384, D).astype(np.float32)
